# version 03
# baseline (speedup 1.0000x reference)
import numpy as np
from contextlib import ExitStack

import concourse.bass as bass
import concourse.mybir as mybir
from concourse.bass_utils import run_bass_kernel_spmd

F32 = mybir.dt.float32
BF16 = mybir.dt.bfloat16
AF = mybir.ActivationFunctionType
ALU = mybir.AluOpType
AX = mybir.AxisListType

B = 8
T = 2048
DM = 1024
NF = 8
DI = 2048
NDI = 16
DH = 64
WIN = 256
DF = 2730
NKC = 22
DEPTH = 6
NT = 16
NS = 4
SS = 512
NEG = -1.0e30

_CFG = dict(layers=DEPTH, add_ssm=True, add_x=True, skip_ffn=False)
_SKIP = set()
_NC_CACHE = {}
_CONST_CACHE = {}


def _host_consts():
    if _CONST_CACHE:
        return _CONST_CACHE
    import ml_dtypes
    try:
        import jax
        import jax.numpy as jnp
        _cpu = jax.devices("cpu")[0]
        with jax.default_device(_cpu):
            half = jnp.arange(0, DH, 2, dtype=jnp.float32)
            inv_freq = 1.0 / 10000.0 ** (half / DH)
            t = jnp.arange(T, dtype=jnp.float32)
            freqs = t[:, None] * inv_freq[None, :]
            freqs = jnp.concatenate([freqs, freqs], -1)
            base_scale = (half + 0.4 * DH) / (1.4 * DH)
            power = (t - T // 2) / (WIN // 2)
            scale = base_scale[None, :] ** power[:, None]
            scale = jnp.concatenate([scale, scale], -1)
            cos = np.asarray(jnp.cos(freqs), dtype=np.float32)
            sin = np.asarray(jnp.sin(freqs), dtype=np.float32)
            scl = np.asarray(scale, dtype=np.float32)
            isc = np.asarray(1.0 / jnp.asarray(scale), dtype=np.float32)
    except Exception:
        half = np.arange(0, DH, 2, dtype=np.float32)
        inv_freq = (1.0 / (np.float32(10000.0) ** (half / np.float32(DH)))).astype(np.float32)
        t = np.arange(T, dtype=np.float32)
        freqs = np.concatenate([t[:, None] * inv_freq] * 2, -1).astype(np.float32)
        base_scale = ((half + np.float32(0.4 * DH)) / np.float32(1.4 * DH)).astype(np.float32)
        power = ((t - T // 2) / np.float32(WIN // 2)).astype(np.float32)
        scl = np.concatenate([(base_scale[None, :] ** power[:, None])] * 2, -1).astype(np.float32)
        cos = np.cos(freqs).astype(np.float32)
        sin = np.sin(freqs).astype(np.float32)
        isc = (np.float32(1.0) / scl).astype(np.float32)

    def dup128(a):
        return np.ascontiguousarray(np.concatenate([a.T, a.T], axis=0).astype(np.float32))

    C = dict(tab_cos=dup128(cos), tab_sin=dup128(sin), tab_scl=dup128(scl), tab_isc=dup128(isc))

    R2 = np.zeros((128, 128), dtype=np.float32)
    for base in (0, 64):
        for d in range(32):
            R2[base + d + 32, base + d] = -1.0
            R2[base + d, base + d + 32] = 1.0
    C["R2"] = R2
    C["IDF"] = np.eye(128, dtype=np.float32)

    msk = np.zeros((3, 128, 384), dtype=np.float32)
    ii = np.arange(128)[:, None]
    jj = np.arange(384)[None, :]
    k0 = jj - 256
    msk[0][(k0 < 0) | (k0 > ii)] = NEG
    k1 = jj - 128
    msk[1][(k1 < 0) | (k1 > 128 + ii)] = NEG
    msk[2][(jj < ii) | (jj > ii + 256)] = NEG
    C["MSK"] = msk

    bd = np.zeros((8, 128, 16), dtype=np.float32)
    for g in range(8):
        bd[g, 0:64, 2 * g] = 1.0
        bd[g, 64:128, 2 * g + 1] = 1.0
    C["BD"] = bd
    sel = np.zeros((8, 16, 128), dtype=np.float32)
    for p in range(8):
        sel[p, 2 * p, 0:64] = 1.0
        sel[p, 2 * p + 1, 64:128] = 1.0
    C["SEL"] = sel

    def hilo(x):
        hi = x.astype(ml_dtypes.bfloat16).astype(np.float64)
        lo = (x - hi).astype(ml_dtypes.bfloat16).astype(np.float64)
        return hi.astype(np.float32), lo.astype(np.float32)

    d9 = np.float64(0.9)
    Tt = np.zeros((128, 128), dtype=np.float64)
    for s in range(128):
        Tt[s, s:] = d9 ** np.arange(128 - s)
    C["T_hi"], C["T_lo"] = hilo(Tt)
    C["dinj_hi"], C["dinj_lo"] = hilo((d9 ** np.arange(1, 129)).reshape(1, 128))
    C["dcol_hi"], C["dcol_lo"] = hilo((d9 ** (127 - np.arange(128))).reshape(128, 1))
    C["p128_hi"], C["p128_lo"] = hilo(np.array([[d9 ** 128]]))
    C["ones_row128"] = np.ones((1, 128), dtype=np.float32)
    C["ones_col"] = np.ones((128, 1), dtype=np.float32)
    C["ones_512"] = np.ones((1, 1024), dtype=np.float32)
    _CONST_CACHE.update(C)
    return _CONST_CACHE


ENGINES = ("pe", "act", "dve", "pool", "sp")


class Prog:
    def __init__(self):
        self.ops = []
        self.last_writer = {}
        self.readers = {}
        self.barrier_deps = []

    def add(self, engine, fn, reads=(), writes=(), dma=False, force_inc=False):
        deps = set(self.barrier_deps)
        for k in reads:
            w = self.last_writer.get(k)
            if w is not None:
                deps.add(w)
        for k in writes:
            w = self.last_writer.get(k)
            if w is not None:
                deps.add(w)
            deps.update(self.readers.get(k, ()))
        i = len(self.ops)
        self.ops.append(dict(engine=engine, fn=fn, deps=deps,
                             inc=16 if dma else 1, force=force_inc or dma))
        for k in reads:
            self.readers.setdefault(k, []).append(i)
        for k in writes:
            self.last_writer[k] = i
            self.readers[k] = []
        return i

    def barrier(self):
        last = {}
        for i, op in enumerate(self.ops):
            last[op["engine"]] = i
        self.barrier_deps = list(last.values())

    def emit(self, block, sems):
        NQ = 8
        n = len(self.ops)
        fdeps = []
        for i, op in enumerate(self.ops):
            ds = set()
            for d in op["deps"]:
                if self.ops[d]["engine"] == "pe" and op["engine"] == "pe":
                    continue
                ds.add(d)
            fdeps.append(ds)
        needed = [op["force"] for op in self.ops]
        for i in range(n):
            for d in fdeps[i]:
                needed[d] = True
        sem_of = [None] * n
        counters = {e: 0 for e in ENGINES}
        qidx = {e: 0 for e in ENGINES}
        pre_wait = [None] * n
        for i, op in enumerate(self.ops):
            e = op["engine"]
            if op["inc"] == 16:
                j = qidx[e]
                qidx[e] = j + 1
                slot = f"{e}q{j % NQ}"
                rnd = j // NQ
                if rnd > 0:
                    pre_wait[i] = (slot, 16 * rnd)
                sem_of[i] = (slot, 16 * (rnd + 1))
            elif needed[i]:
                counters[e] += 1
                sem_of[i] = (e, counters[e])
        self.totals = dict(counters)
        self.qtotals = dict(qidx)

        def make_body(ename):
            my_ops = [(i, op) for i, op in enumerate(self.ops) if op["engine"] == ename]

            def body(eng):
                hw = {}
                def wait(sname, v):
                    if v > hw.get(sname, 0):
                        eng.wait_ge(sems[sname], v)
                        hw[sname] = v
                for i, op in my_ops:
                    waits = {}
                    for d in fdeps[i]:
                        sname, v = sem_of[d]
                        if v > waits.get(sname, 0):
                            waits[sname] = v
                    if pre_wait[i] is not None:
                        sname, v = pre_wait[i]
                        if v > waits.get(sname, 0):
                            waits[sname] = v
                    for sname in sorted(waits):
                        wait(sname, waits[sname])
                    inst = op["fn"](eng)
                    if sem_of[i] is not None:
                        sname, v = sem_of[i]
                        inst.then_inc(sems[sname], 16 if op["inc"] == 16 else 1)
                if ename == "sp":
                    for j in range(min(NQ, self.qtotals.get("sp", 0))):
                        tot = self.qtotals["sp"]
                        full = 16 * ((tot - 1 - j) // NQ + 1) if j < tot else 0
                        wait(f"spq{j}", full)
                    for j in range(min(NQ, self.qtotals.get("pool", 0))):
                        tot = self.qtotals["pool"]
                        full = 16 * ((tot - 1 - j) // NQ + 1) if j < tot else 0
                        wait(f"poolq{j}", full)
            return body

        block.tensor(make_body("pe"))
        block.scalar(make_body("act"))
        block.vector(make_body("dve"))
        block.gpsimd(make_body("pool"))
        block.sync(make_body("sp"))


def copy_to(eng, out, in_):
    if hasattr(eng, "tensor_copy"):
        return eng.tensor_copy(out, in_)
    return eng.copy(out, in_)


def carve(arena, off, shape):
    n = int(np.prod(shape))
    ap_ = arena[:, off:off + n]
    if len(shape) > 1:
        names = "abcd"[: len(shape)]
        spec = "p (" + " ".join(names) + ") -> p " + " ".join(names)
        kw = {names[i]: shape[i] for i in range(len(shape))}
        ap_ = ap_.rearrange(spec, **kw)
    return ap_, off + n


def build(cfg_key):
    layers, add_ssm, add_x, skip_ffn = cfg_key
    nc = bass.Bass("TRN2", target_bir_lowering=False, debug=False)

    D = {}
    def din(name, shape):
        D[name] = nc.dram_tensor(name, list(shape), F32, kind="ExternalInput").ap()
    din("x", (T, DM))
    din("Win", (DM, DI)); din("bin", (DI,)); din("Bp", (DI, 16)); din("Cp", (16, DI))
    din("Dp", (DI,)); din("Wout", (DI, DM)); din("bout", (DM,))
    din("Wqkv", (DEPTH, DM, 3 * DM)); din("qsc", (DEPTH, DH)); din("ksc", (DEPTH, DH))
    din("Wo", (DEPTH, DM, DM)); din("lng", (DEPTH, DM)); din("lnb", (DEPTH, DM))
    din("W1", (DEPTH, DM, 2 * DF)); din("b1", (DEPTH, 2 * DF))
    din("W2", (DEPTH, DF, DM)); din("b2", (DEPTH, DM))
    for k in ("tab_cos", "tab_sin", "tab_scl", "tab_isc"):
        din(k, (128, T))
    din("R2", (128, 128)); din("IDF", (128, 128)); din("MSK", (3, 128, 384))
    din("BD", (8, 128, 16)); din("SEL", (8, 16, 128))
    din("T_hi", (128, 128)); din("T_lo", (128, 128))
    din("dinj_hi", (1, 128)); din("dinj_lo", (1, 128))
    din("dcol_hi", (128, 1)); din("dcol_lo", (128, 1))
    din("p128_hi", (1, 1)); din("p128_lo", (1, 1))
    din("ones_row128", (1, 128)); din("ones_col", (128, 1)); din("ones_512", (1, 1024))

    ssm_scr = nc.dram_tensor("ssm_scr", [NF, 128, T], F32, kind="Internal").ap()
    tab_scr = nc.dram_tensor("tab_scr", [4, 128, T], BF16, kind="Internal").ap()
    OUT = nc.dram_tensor("out", [T, DM], F32, kind="ExternalOutput").ap()

    ctx = ExitStack()
    E = ctx.enter_context

    Rb = E(nc.sbuf_tensor("Rb", [128, NF, 2049], F32))
    AA = E(nc.sbuf_tensor("AA", [128, 24576], BF16))
    KR = E(nc.sbuf_tensor("KR", [128, NF, 1024], BF16))
    ARX = E(nc.sbuf_tensor("ARX", [128, 22528], BF16))
    BIGA = E(nc.sbuf_tensor("BIGA", [128, NF, 512], BF16))
    WR = E(nc.sbuf_tensor("WR", [128, 4096], BF16))
    OUTB = E(nc.sbuf_tensor("OUTB", [128, 1024], F32))
    SCR = E(nc.sbuf_tensor("SCR", [128, 2, 512], BF16))
    STATS = E(nc.sbuf_tensor("STATS", [128, 1024], F32))
    INV = E(nc.sbuf_tensor("INV", [64, 512], F32))
    IVQC = E(nc.sbuf_tensor("IVQC", [128, 4, 16], F32))
    NMX = E(nc.sbuf_tensor("NMX", [128, 8], F32))
    SUMS = E(nc.sbuf_tensor("SUMS", [128, 16], F32))
    OSC = E(nc.sbuf_tensor("OSC", [128, 16], F32))
    QKCOL = E(nc.sbuf_tensor("QKCOL", [128, 2], F32))
    BCOL = E(nc.sbuf_tensor("BCOL", [128, 64], F32))
    ETMP = STATS[:, 768:784]
    B1A = BCOL[:, 0:22]
    B1G = BCOL[:, 22:44]
    LNGB = BCOL[:, 44:60]
    R2B = E(nc.sbuf_tensor("R2B", [128, 128], BF16))
    IDFB = E(nc.sbuf_tensor("IDFB", [128, 128], F32))
    IDB = E(nc.sbuf_tensor("IDB", [128, 128], BF16))
    MSKF = E(nc.sbuf_tensor("MSKF", [128, 3, 384], BF16))
    BDB = E(nc.sbuf_tensor("BDB", [128, 8, 16], BF16))
    SELB = E(nc.sbuf_tensor("SELB", [48, 8, 128], BF16))
    THI = E(nc.sbuf_tensor("THI", [128, 128], BF16))
    TLO = E(nc.sbuf_tensor("TLO", [128, 128], BF16))
    DINJB = E(nc.sbuf_tensor("DINJB", [1, 256], BF16))
    DINJ = DINJB[:, 0:128]
    DINJ2 = DINJB[:, 128:256]
    DCOL = E(nc.sbuf_tensor("DCOL", [128, 2], BF16))
    P128 = E(nc.sbuf_tensor("P128", [1, 2], BF16))
    ONER = E(nc.sbuf_tensor("ONER", [1, 128], F32))
    ONECB = E(nc.sbuf_tensor("ONECB", [128, 1], BF16))

    rem = nc.sbuf_bytes_remaining
    assert rem >= 0, f"SBUF over budget by {-rem} bytes"

    off = 0
    VR, off = carve(ARX, off, (6, 1024))
    TABR, off = carve(ARX, off, (4, 512))
    QPR, off = carve(ARX, off, (8, 512))
    QSR, off = carve(ARX, off, (2, 512))
    KSR, off = carve(ARX, off, (2, 512))
    T2R, off = carve(ARX, off, (2, 512))
    IVB, off = carve(ARX, off, (2, 512))
    PSBr, off = carve(ARX, off, (2, 2, 384))
    PTBr, off = carve(ARX, off, (2, 2, 384))
    OFMv, off = carve(ARX, off, (8, 256))
    assert off <= 22528, off
    W2R, _o = carve(ARX, 0, (NKC, 1024))
    assert _o <= 22528
    off = 0
    XIF, off = carve(ARX, off, (16, 256))
    XIT, off = carve(ARX, off, (2, 2048))
    YF, off = carve(ARX, off, (16, 256))
    CEb, off = carve(ARX, off, (16, 256))
    assert off <= 22528

    WQ = AA.rearrange("p (a b) -> p a b", a=NF)
    MH = AA[:, 0:NKC * 1024].rearrange("p (k c) -> p k c", k=NKC)
    WINV = AA[:, 0:16384].rearrange("p (a b) -> p a b", a=NF)
    WOUTA = AA[:, 16384:24576].rearrange("p (a b) -> p a b", a=8)
    WOUTB = KR.rearrange("p a b -> p (a b)").rearrange("p (a b) -> p a b", a=8)
    XNH = KR
    WOB = WR[:, 0:2048].rearrange("p (r c) -> p r c", r=2)
    R2QS = WR[:, 2048:2176]
    R2KS = WR[:, 2176:2304]
    W1R = WR.rearrange("p (r a j c) -> p r a j c", r=2, a=2, j=8)
    CPST = OUTB[0:16, 0:128]
    BPST = OUTB[:, 128:144]

    PS = [E(nc.psum_tensor(f"PS{i}", [128, 512], F32)) for i in range(6)]
    PB = [E(nc.psum_tensor(f"PBb{i}", [128, 768], BF16)) for i in range(2)]

    sems = {e: E(nc.semaphore(f"sem_{e}")) for e in ENGINES}
    for e in ("sp", "pool"):
        for j in range(8):
            sems[f"{e}q{j}"] = E(nc.semaphore(f"sem_{e}q{j}"))
    block = E(nc.Block())
    P = Prog()

    def dmaop(q, dst, src, wkey, rkeys=(), force=False):
        wk = (wkey,) if isinstance(wkey, str) else tuple(wkey)
        P.add(q, lambda eng, d=dst, s=src: eng.dma_start(out=d, in_=s),
              reads=rkeys, writes=wk, dma=True, force_inc=force)

    if "consts" in _SKIP:
        def dmaop_consts(*a, **k):
            pass
    else:
        dmaop_consts = dmaop
    dmaop = dmaop if "consts" not in _SKIP else dmaop_consts
    _real_dmaop = dmaop
    dmaop_c = dmaop_consts
    dmaop_c("pool", R2B[:], D["R2"], "R2B")
    dmaop_c("pool", IDB[:], D["IDF"], "IDB")
    dmaop_c("sp", IDFB[:], D["IDF"], "IDF")
    dmaop_c("pool", MSKF[:], D["MSK"].rearrange("k p n -> p k n"), "MSKF")
    dmaop_c("pool", BDB[:], D["BD"].rearrange("k p n -> p k n"), "BDB")
    dmaop_c("pool", SELB[32:48, :, :], D["SEL"].rearrange("k p n -> p k n"), "SELB")
    dmaop_c("pool", THI[:], D["T_hi"], "THI")
    dmaop_c("pool", TLO[:], D["T_lo"], "TLO")
    dmaop_c("pool", DINJ[:], D["dinj_hi"], "DINJ")
    dmaop_c("pool", DINJ2[:], D["dinj_lo"], "DINJ2")
    dmaop_c("pool", DCOL[:, 0:1], D["dcol_hi"], "DCOL")
    dmaop_c("pool", DCOL[:, 1:2], D["dcol_lo"], "DCOLb")
    dmaop_c("pool", P128[0:1, 0:1], D["p128_hi"], "P128")
    dmaop_c("pool", P128[0:1, 1:2], D["p128_lo"], "P128b")
    dmaop_c("sp", ONER[:], D["ones_row128"], "ONER")
    dmaop_c("pool", ONECB[:], D["ones_col"], "ONECB")

    for idx, (a_name, b_name) in enumerate(() if "tab" in _SKIP else
                                           (("tab_cos", "tab_scl"), ("tab_sin", "tab_scl"),
                                            ("tab_cos", "tab_isc"), ("tab_sin", "tab_isc"))):
        for q4 in range(4):
            cols = slice(q4 * 512, (q4 + 1) * 512)
            sl = q4 % 2
            dmaop("sp", OUTB[:, 0:512], D[a_name][:, cols], ("tca", "OUTB"), rkeys=("tcp",))
            dmaop("sp", OUTB[:, 512:1024], D[b_name][:, cols], ("tcb", "OUTB"), rkeys=("tcp",))
            P.add("dve", lambda eng, sl=sl: eng.tensor_tensor(
                SCR[:, sl, :], OUTB[:, 0:512], OUTB[:, 512:1024], ALU.mult),
                reads=("tca", "tcb", "OUTB"), writes=("tcp", f"SCR{sl}"))
            P.add("sp", lambda eng, idx=idx, c=cols, sl=sl: eng.dma_start(
                out=tab_scr[idx, :, c], in_=SCR[:, sl, :]),
                reads=(f"SCR{sl}",), writes=(f"tdone{idx}{q4}",), dma=True)
    P.barrier()

    P.add("dve", lambda eng: eng.memset(Rb[:, :, 0:1], 0.0), writes=("Rz",))
    for j in range(0 if "xload" in _SKIP else NT):
        dmaop("sp", OUTB[:], D["x"][j * 128:(j + 1) * 128, :], ("xld", "OUTB"), rkeys=("xcons",))
        for fg in range(2):
            bank = f"PS{fg}"
            for ff in range(4):
                f = fg * 4 + ff
                P.add("pe", lambda eng, f=f, fg=fg, ff=ff: eng.transpose(
                    PS[fg][:, ff * 128:(ff + 1) * 128],
                    OUTB[:, f * 128:(f + 1) * 128], IDFB[:]),
                    reads=("xld", "IDF", "OUTB"), writes=(bank,))
            for ff in range(4):
                f = fg * 4 + ff
                def cp(eng, f=f, j=j, fg=fg, ff=ff):
                    return copy_to(eng, Rb[:, f, 1 + j * 128:1 + (j + 1) * 128],
                                   PS[fg][:, ff * 128:(ff + 1) * 128])
                wk = [f"R{f}.{j // 4}"]
                if fg == 1 and ff == 3:
                    wk.append("xcons")
                P.add("dve", cp, reads=(bank,), writes=tuple(wk))
    P.barrier()

    SL = lambda s: slice(1 + s * SS, 1 + (s + 1) * SS)

    def shifted_cols(f, s, n=SS):
        o = (0 if f >= 4 else 1) + s * n
        return slice(o, o + n)

    def rk_shift(f, s):
        if f >= 4 and s > 0:
            return (f"R{f}.{s}", f"R{f}.{s - 1}")
        return (f"R{f}.{s}",)


    def emit_ln_stats_rows(s, shifted, rk):
        for f in range(NF):
            fr = f % 2
            cols = shifted_cols(f, s) if shifted else SL(s)
            P.add("dve", lambda eng, f=f, cols=cols, fr=fr: eng.tensor_copy(
                SCR[:, fr, :], Rb[:, f, cols]),
                reads=(rk_shift(f, s) if shifted else (f"R{f}.{s}",)), writes=(f"SCR{fr}",))
            P.add("pe", lambda eng, f=f, fr=fr: eng.matmul(
                PS[5][0:1, 0:512], ONECB[:], SCR[:, fr, :],
                start=(f == 0), stop=(f == NF - 1)),
                reads=(f"SCR{fr}", "ONECB"), writes=("PS5",))
        for f in range(NF):
            fr = f % 2
            cols = shifted_cols(f, s) if shifted else SL(s)
            P.add("act", lambda eng, f=f, cols=cols, fr=fr: eng.activation(
                SCR[:, fr, :], Rb[:, f, cols], AF.Square),
                reads=(rk_shift(f, s) if shifted else (f"R{f}.{s}",)), writes=(f"SCR{fr}",))
            P.add("pe", lambda eng, f=f, fr=fr: eng.matmul(
                PS[4][0:1, 0:512], ONECB[:], SCR[:, fr, :],
                start=(f == 0), stop=(f == NF - 1)),
                reads=(f"SCR{fr}", "ONECB"), writes=("PS4",))
        P.add("dve", lambda eng: eng.tensor_scalar(
            STATS[0:1, 0:512], PS[5][0:1, 0:512], 1.0 / DM, None, ALU.mult),
            reads=("PS5",), writes=("ST0",))
        P.add("dve", lambda eng: eng.tensor_scalar(
            STATS[0:1, 512:1024], PS[4][0:1, 0:512], 1.0 / DM, None, ALU.mult),
            reads=("PS4",), writes=("ST0b",))
        P.add("dve", lambda eng: eng.tensor_tensor(
            INV[0:1, 0:512], STATS[0:1, 0:512], STATS[0:1, 0:512], ALU.mult),
            reads=("ST0",), writes=("IV0",))
        P.add("dve", lambda eng: eng.tensor_tensor(
            STATS[0:1, 512:1024], STATS[0:1, 512:1024], INV[0:1, 0:512], ALU.subtract),
            reads=("IV0",), writes=("ST0b",))
        P.add("dve", lambda eng: eng.tensor_scalar(
            STATS[0:1, 512:1024], STATS[0:1, 512:1024], 1e-5, None, ALU.add),
            reads=(), writes=("ST0b",))
        P.add("act", lambda eng: eng.activation(
            INV[0:1, 0:512], STATS[0:1, 512:1024], AF.Ln),
            reads=("ST0b",), writes=("IV0",))
        P.add("act", lambda eng: eng.activation(
            STATS[0:1, 512:1024], INV[0:1, 0:512], AF.Exp, scale=np.float32(-0.5)),
            reads=("IV0",), writes=("ST0b",))
        P.add("pe", lambda eng: eng.matmul(
            PS[4][:, 0:512], ONER[:], STATS[0:1, 0:512], start=True, stop=True),
            reads=("ST0", "ONER"), writes=("PS4",))
        P.add("pe", lambda eng: eng.matmul(
            PS[5][:, 0:512], ONER[:], STATS[0:1, 512:1024], start=True, stop=True),
            reads=("ST0b", "ONER"), writes=("PS5",))

    for s in range(() and 0 or (0 if "ln" in _SKIP else NS)):
        emit_ln_stats_rows(s, shifted=False, rk=f"ol{s}")
        for f in range(NF):
            P.add("dve", lambda eng, f=f, s=s: eng.tensor_tensor(
                Rb[:, f, SL(s)], Rb[:, f, SL(s)], PS[4][:, 0:512], ALU.subtract),
                reads=("PS4", f"R{f}.{s}"), writes=(f"R{f}.{s}",))
            P.add("dve", lambda eng, f=f, s=s: eng.tensor_tensor(
                Rb[:, f, SL(s)], Rb[:, f, SL(s)], PS[5][:, 0:512], ALU.mult),
                reads=("PS5", f"R{f}.{s}"), writes=(f"R{f}.{s}",))
    P.barrier()

    if add_ssm:
        for g in range(NF):
            dmaop("pool", WINV[:, g, :], D["Win"][g * 128:(g + 1) * 128, :], f"WIN{g}")
        for g in range(NDI):
            tgt = WOUTA[:, g, :] if g < 8 else WOUTB[:, g - 8, :]
            dmaop("pool", tgt, D["Wout"][g * 128:(g + 1) * 128, :], f"WOUT{g}")
        for nm, nrow, coff in (("bin", 16, 0), ("Dp", 16, 16), ("bout", 8, 32)):
            dmaop("sp", OUTB[0:nrow, 0:128], D[nm].rearrange("(a b) -> a b", a=nrow),
                  (f"rw{nm}", "OUTB"), rkeys=("PS0",))
            P.add("pe", lambda eng, nm=nm, nrow=nrow: eng.transpose(
                PS[0][:, 0:nrow], OUTB[0:nrow, 0:128], IDFB[0:nrow, 0:nrow]),
                reads=(f"rw{nm}", "IDF", "OUTB"), writes=("PS0",))
            P.add("dve", lambda eng, nrow=nrow, coff=coff: eng.tensor_copy(
                BCOL[:, coff:coff + nrow], PS[0][:, 0:nrow]),
                reads=("PS0",), writes=(f"col{coff}",))
        for g in range(NDI):
            dmaop("sp", CPST, D["Cp"][:, g * 128:(g + 1) * 128], ("cpst", "OUTB"), rkeys=("PS1",))
            dmaop("sp", BPST, D["Bp"][g * 128:(g + 1) * 128, :], ("bpst", "OUTB"), rkeys=("etmp",))
            P.add("pe", lambda eng: eng.transpose(
                PS[1][:, 0:16], CPST, IDFB[0:16, 0:16]),
                reads=("cpst", "IDF", "OUTB"), writes=("PS1",))
            P.add("dve", lambda eng: eng.tensor_tensor(
                ETMP[:, :], PS[1][:, 0:16], BPST, ALU.mult),
                reads=("PS1", "bpst", "OUTB"), writes=("etmp",))
            P.add("dve", lambda eng, g=g: eng.reduce_sum(
                BCOL[:, 40 + g:41 + g], ETMP[:, :], axis=AX.X),
                reads=("etmp",), writes=(f"ecol{g}",))
        P.add("dve", lambda eng: eng.memset(WR[0:1, :], 0.0), writes=("car",))
        P.barrier()

        SSS = 256
        for s in range(8):
            s4 = s // 2
            for f in range(NF):
                P.add("pool", lambda eng, f=f, s=s: eng.tensor_copy(
                    BIGA[:, f, 0:SSS] if s % 2 == 0 else BIGA[:, f, 256:512],
                    Rb[:, f, slice(1 + s * SSS, 1 + (s + 1) * SSS)]),
                    reads=(f"R{f}.{s4}",), writes=(f"scst{f}",))
            bsl = (lambda f: BIGA[:, f, 0:SSS]) if s % 2 == 0 else (lambda f: BIGA[:, f, 256:512])
            for g in range(NDI):
                bank = f"PS{g % 2}"
                for j in range(NF):
                    P.add("pe", lambda eng, g=g, j=j, bsl=bsl: eng.matmul(
                        PS[g % 2][:, 0:SSS], WINV[:, j, g * 128:(g + 1) * 128],
                        bsl(j), start=(j == 0), stop=(j == NF - 1)),
                        reads=(f"WIN{j}", f"scst{j}"), writes=(bank,))
                P.add("dve", lambda eng, g=g: eng.tensor_scalar(
                    SCR[:, g % 2, 0:SSS], PS[g % 2][:, 0:SSS],
                    BCOL[:, g:g + 1], None, ALU.add),
                    reads=(bank, "col0"), writes=(f"SCR{g % 2}",))
                P.add("act", lambda eng, g=g: eng.activation(
                    XIF[:, g, :], SCR[:, g % 2, 0:SSS], AF.Silu),
                    reads=(f"SCR{g % 2}",), writes=(f"xif{g}",))
            for c in range(2):
                for w in range(3):
                    lo, hi = w * 6, min(NDI, w * 6 + 6)
                    bank = f"PB{w % 2}"
                    for gi, g in enumerate(range(lo, hi)):
                        P.add("pe", lambda eng, g=g, c=c, gi=gi, w=w: eng.transpose(
                            PB[w % 2][:, gi * 128:(gi + 1) * 128],
                            XIF[:, g, c * 128:(c + 1) * 128], IDB[:]),
                            reads=(f"xif{g}", "IDB"), writes=(bank,))
                    nn = hi - lo
                    P.add("dve", lambda eng, c=c, w=w, nn=nn, lo=lo: eng.tensor_copy(
                        XIT[:, c, lo * 128:(lo + nn) * 128], PB[w % 2][:, 0:nn * 128]),
                        reads=(bank,), writes=(f"xit{c}",))
            for c in range(2):
                cg = s * 2 + c
                cur, nxt = (cg % 2) * 2048, ((cg + 1) % 2) * 2048
                for g in range(NDI):
                    bank = f"PS{2 + g % 2}"
                    P.add("pe", lambda eng, g=g, c=c: eng.matmul(
                        PS[2 + g % 2][:, 0:128], XIT[:, c, g * 128:(g + 1) * 128], THI[:],
                        start=True, stop=False),
                        reads=(f"xit{c}", "THI"), writes=(bank,))
                    P.add("pe", lambda eng, g=g, c=c, last=(cg == 0): eng.matmul(
                        PS[2 + g % 2][:, 0:128], XIT[:, c, g * 128:(g + 1) * 128], TLO[:],
                        start=False, stop=last),
                        reads=(f"xit{c}", "TLO"), writes=(bank,))
                    if cg > 0:
                        crow = WR[0:1, cur:cur + 2048]
                        P.add("pe", lambda eng, g=g, crow=crow: eng.matmul(
                            PS[2 + g % 2][:, 0:128], crow[:, g * 128:(g + 1) * 128],
                            DINJ[:], start=False, stop=False),
                            reads=("car", "DINJ"), writes=(bank,))
                        P.add("pe", lambda eng, g=g, crow=crow: eng.matmul(
                            PS[2 + g % 2][:, 0:128], crow[:, g * 128:(g + 1) * 128],
                            DINJ2[:], start=False, stop=True),
                            reads=("car", "DINJ2"), writes=(bank,))
                    P.add("dve", lambda eng, g=g, c=c: eng.tensor_scalar(
                        CEb[:, g, c * 128:(c + 1) * 128], PS[2 + g % 2][:, 0:128],
                        BCOL[:, 40 + g:41 + g], None, ALU.mult),
                        reads=(bank, f"ecol{g}"), writes=(f"ce{g}",))
                    P.add("dve", lambda eng, g=g, c=c: eng.scalar_tensor_tensor(
                        YF[:, g, c * 128:(c + 1) * 128], XIF[:, g, c * 128:(c + 1) * 128],
                        BCOL[:, 16 + g:17 + g], CEb[:, g, c * 128:(c + 1) * 128],
                        ALU.mult, ALU.add),
                        reads=(f"xif{g}", "col16", f"ce{g}"), writes=(f"yf{g}",))
                cbanks = ((PS[5], "PS5"), (PS[1], "PS1"), (PS[0], "PS0"), (PS[4], "PS4"))
                for q in range(4):
                    cb, cbk = cbanks[q]
                    P.add("pe", lambda eng, q=q, c=c, cb=cb: eng.matmul(
                        cb[0:1, 0:512], DCOL[:, 0:1],
                        XIT[:, c, q * 512:(q + 1) * 512], start=True, stop=False),
                        reads=(f"xit{c}", "DCOL"), writes=(cbk,))
                    P.add("pe", lambda eng, q=q, c=c, cb=cb,
                          last=(cg == 0): eng.matmul(
                        cb[0:1, 0:512], DCOL[:, 1:2],
                        XIT[:, c, q * 512:(q + 1) * 512], start=False, stop=last),
                        reads=(f"xit{c}", "DCOLb"), writes=(cbk,))
                    if cg > 0:
                        srow = (lambda q=q, cur=cur: WR[0:1, cur + q * 512:cur + (q + 1) * 512])
                        P.add("pe", lambda eng, srow=srow, cb=cb: eng.matmul(
                            cb[0:1, 0:512], P128[0:1, 0:1], srow(),
                            start=False, stop=False),
                            reads=("car", "P128"), writes=(cbk,))
                        P.add("pe", lambda eng, srow=srow, cb=cb: eng.matmul(
                            cb[0:1, 0:512], P128[0:1, 1:2], srow(),
                            start=False, stop=True),
                            reads=("car", "P128b"), writes=(cbk,))
                for q in range(4):
                    cb, cbk = cbanks[q]
                    dstrow = (lambda q=q, nxt=nxt: WR[0:1, nxt + q * 512:nxt + (q + 1) * 512])
                    P.add("dve", lambda eng, dstrow=dstrow, cb=cb: eng.tensor_copy(
                        dstrow(), cb[0:1, 0:512]),
                        reads=(cbk,), writes=("car",))
            for m in range(NF):
                pt = PS[4] if m % 2 == 0 else PS[0]
                bank = "PS4" if m % 2 == 0 else "PS0"
                for g in range(NDI):
                    w = WOUTA[:, g, :] if g < 8 else WOUTB[:, g - 8, :]
                    P.add("pe", lambda eng, m=m, g=g, w=w, pt=pt: eng.matmul(
                        pt[:, 0:SSS], w[:, m * 128:(m + 1) * 128], YF[:, g, :],
                        start=(g == 0), stop=(g == NDI - 1)),
                        reads=(f"yf{g}", f"WOUT{g}"), writes=(bank,))
                sl = m % 2
                P.add("dve", lambda eng, m=m, pt=pt, sl=sl: eng.tensor_scalar(
                    STATS[:, sl * 256:(sl + 1) * 256], pt[:, 0:SSS],
                    BCOL[:, 32 + m:33 + m], None, ALU.add),
                    reads=(bank, "col32", f"ssmst{sl}"), writes=(f"sso{sl}",))
                P.add("sp", lambda eng, m=m, s=s, sl=sl: eng.dma_start(
                    out=ssm_scr[m, :, s * SSS:(s + 1) * SSS],
                    in_=STATS[:, sl * 256:(sl + 1) * 256]),
                    reads=(f"sso{sl}",), writes=(f"ssmst{sl}",), dma=True)
        P.barrier()

    def emit_attn(l):
        for j in range(NF):
            dmaop("pool", WQ[:, j, :], D["Wqkv"][l, j * 128:(j + 1) * 128, :], f"WQ{j}")
        dmaop("sp", QKCOL[0:64, 0:1], D["qsc"][l].rearrange("(a b) -> a b", a=64), "qs1")
        dmaop("sp", QKCOL[64:128, 0:1], D["qsc"][l].rearrange("(a b) -> a b", a=64), "qs2")
        dmaop("sp", QKCOL[0:64, 1:2], D["ksc"][l].rearrange("(a b) -> a b", a=64), "ks1")
        dmaop("sp", QKCOL[64:128, 1:2], D["ksc"][l].rearrange("(a b) -> a b", a=64), "ks2")
        qsk = ("qs1", "qs2")
        ksk = ("ks1", "ks2")
        P.add("dve", lambda eng: eng.tensor_scalar(
            R2QS, R2B[:], QKCOL[:, 0:1], None, ALU.mult),
            reads=("R2B", "qs1", "qs2"), writes=("R2QS",))
        P.add("dve", lambda eng: eng.tensor_scalar(
            R2KS, R2B[:], QKCOL[:, 1:2], None, ALU.mult),
            reads=("R2B", "ks1", "ks2"), writes=("R2KS",))

        for s in range(NS):
            for ti in range(4):
                dmaop("sp", TABR[:, ti, :], tab_scr[ti, :, s * SS:(s + 1) * SS],
                      f"tab{ti}", rkeys=("tabdone",))
            castk = tuple(f"acst{f}" for f in range(NF))
            for f in range(NF):
                P.add("pool", lambda eng, f=f, s=s: eng.tensor_copy(
                    BIGA[:, f, :], Rb[:, f, shifted_cols(f, s)]),
                    reads=rk_shift(f, s), writes=(f"acst{f}",))

            for which in ("q", "k"):
                base = 0 if which == "q" else 1024
                ntgt = PS[4] if which == "q" else PS[5]
                nbank = "PS4" if which == "q" else "PS5"
                irow = 0 if which == "q" else 32
                ikey = "invq" if which == "q" else "invk"
                scol = QKCOL[:, 0:1] if which == "q" else QKCOL[:, 1:2]
                sck = qsk if which == "q" else ksk
                evr = QSR if which == "q" else KSR
                evk = "qsr" if which == "q" else "ksr"
                rot_l = R2QS if which == "q" else R2KS
                rot_k = "R2QS" if which == "q" else "R2KS"
                def proj_stage(g, base=base, evr=evr, evk=evk):
                    bank = f"PS{g % 2}"
                    for j in range(NF):
                        P.add("pe", lambda eng, g=g, j=j, base=base: eng.matmul(
                            PS[g % 2][:, 0:512], WQ[:, j, base + g * 128:base + (g + 1) * 128],
                            BIGA[:, j, :], start=(j == 0), stop=(j == NF - 1)),
                            reads=(f"WQ{j}", castk[j]), writes=(bank,))
                    P.add("dve", lambda eng, g=g, evr=evr: eng.tensor_copy(
                        evr[:, g % 2, :], PS[g % 2][:, 0:512]),
                        reads=(bank,), writes=(f"{evk}{g % 2}",))

                def post_stage(g, which=which, evr=evr, evk=evk, ntgt=ntgt,
                               nbank=nbank, irow=irow, rot_l=rot_l, rot_k=rot_k,
                               scol=scol, sck=sck, s=s):
                    P.add("act", lambda eng, g=g, evr=evr: eng.activation(
                        SCR[:, g % 2, :], evr[:, g % 2, :], AF.Square),
                        reads=(f"{evk}{g % 2}",), writes=(f"SCR{g % 2}",))
                    P.add("pe", lambda eng, g=g, ntgt=ntgt, irow=irow: eng.matmul(
                        ntgt[irow:irow + 16, 0:512], BDB[:, g, :], SCR[:, g % 2, :],
                        start=(g == 0), stop=(g == NF - 1)),
                        reads=(f"SCR{g % 2}", "BDB"), writes=(nbank,))
                    P.add("pe", lambda eng, g=g, rot_l=rot_l, evr=evr: eng.matmul(
                        PS[2][:, 0:512], rot_l, evr[:, g % 2, :], start=True, stop=True),
                        reads=(rot_k, f"{evk}{g % 2}"), writes=("PS2",))
                    tsin = 1 if which == "q" else 3
                    tcos = 0 if which == "q" else 2
                    P.add("dve", lambda eng, g=g, tsin=tsin: eng.tensor_tensor(
                        T2R[:, g % 2, :], PS[2][:, 0:512], TABR[:, tsin, :], ALU.mult),
                        reads=("PS2", f"tab{tsin}"), writes=(f"t2{g % 2}",))
                    if which == "q":
                        P.add("dve", lambda eng, g=g, tcos=tcos, scol=scol, evr=evr:
                              eng.scalar_tensor_tensor(
                                  QPR[:, g, :], evr[:, g % 2, :], scol, TABR[:, tcos, :],
                                  ALU.mult, ALU.mult),
                              reads=(f"{evk}{g % 2}", f"tab{tcos}") + sck, writes=(f"qp{g}",))
                        P.add("pool", lambda eng, g=g: eng.tensor_tensor(
                            QPR[:, g, :], QPR[:, g, :], T2R[:, g % 2, :], ALU.add),
                            reads=(f"t2{g % 2}", f"qp{g}"), writes=(f"qp{g}",))
                    else:
                        kp = f"kp{g}_{s % 2}"
                        kcols = slice((s % 2) * 512, (s % 2) * 512 + 512)
                        P.add("dve", lambda eng, g=g, kcols=kcols, tcos=tcos, scol=scol, evr=evr:
                              eng.scalar_tensor_tensor(
                                  KR[:, g, kcols], evr[:, g % 2, :], scol, TABR[:, tcos, :],
                                  ALU.mult, ALU.mult),
                              reads=(f"{evk}{g % 2}", f"tab{tcos}") + sck, writes=(kp,))
                        P.add("pool", lambda eng, g=g, kcols=kcols: eng.tensor_tensor(
                            KR[:, g, kcols], KR[:, g, kcols], T2R[:, g % 2, :], ALU.add),
                            reads=(f"t2{g % 2}", kp), writes=(kp,))

                proj_stage(0)
                for g in range(1, NF):
                    proj_stage(g)
                    post_stage(g - 1)
                post_stage(NF - 1)
                P.add("dve", lambda eng, ntgt=ntgt, irow=irow: eng.tensor_scalar(
                    INV[irow:irow + 16, :], ntgt[irow:irow + 16, 0:512], 1e-30, None, ALU.add),
                    reads=(nbank,), writes=(ikey,))
                P.add("act", lambda eng, irow=irow: eng.activation(
                    INV[irow:irow + 16, :], INV[irow:irow + 16, :], AF.Ln),
                    reads=(ikey,), writes=(ikey,))
                P.add("act", lambda eng, irow=irow: eng.activation(
                    INV[irow:irow + 16, :], INV[irow:irow + 16, :], AF.Exp, scale=np.float32(0.5)),
                    reads=(ikey,), writes=(ikey,))
                P.add("dve", lambda eng, irow=irow: eng.tensor_scalar(
                    INV[irow:irow + 16, :], INV[irow:irow + 16, :], 1e-12, None, ALU.max),
                    reads=(ikey,), writes=(ikey,))
                P.add("dve", lambda eng, irow=irow: eng.reciprocal(
                    INV[irow:irow + 16, :], INV[irow:irow + 16, :]),
                    reads=(ikey,), writes=(ikey,))
            P.add("dve", lambda eng: eng.tensor_copy(
                SCR[32:48, 0, :], INV[32:48, :]),
                reads=("invk",), writes=("SCR0",))
            for g in range(NF):
                pt = PS[4] if g % 2 == 0 else PS[5]
                bank = "PS4" if g % 2 == 0 else "PS5"
                kp = f"kp{g}_{s % 2}"
                kcols = slice((s % 2) * 512, (s % 2) * 512 + 512)
                P.add("pe", lambda eng, g=g, pt=pt: eng.matmul(
                    pt[:, 0:512], SELB[32:48, g, :], SCR[32:48, 0, :], start=True, stop=True),
                    reads=("SCR0", "SELB"), writes=(bank,))
                P.add("dve", lambda eng, g=g, pt=pt: eng.tensor_copy(
                    IVB[:, g % 2, :], pt[:, 0:512]),
                    reads=(bank,), writes=(f"ivb{g % 2}",))
                P.add("pool", lambda eng, g=g, kcols=kcols: eng.tensor_tensor(
                    KR[:, g, kcols], KR[:, g, kcols], IVB[:, g % 2, :], ALU.mult),
                    reads=(f"ivb{g % 2}", kp), writes=(kp,))
            for c in range(4):
                P.add("pe", lambda eng, c=c: eng.transpose(
                    PS[4][:, 0:16], INV[0:16, c * 128:(c + 1) * 128], IDFB[0:16, 0:16]),
                    reads=("invq", "IDF"), writes=("PS4",))
                P.add("dve", lambda eng, c=c: eng.tensor_scalar(
                    IVQC[:, c, :], PS[4][:, 0:16], 8.0, None, ALU.mult),
                    reads=("PS4",), writes=(f"ivqc{c}",))
            P.add("dve", lambda eng: eng.engine_nop(),
                  reads=("tab0", "tab1", "tab2", "tab3"), writes=("tabdone",))

            for c in range(4):
                tg = s * 4 + c
                for vc in range(2):
                    bank = f"PS{vc}"
                    for j in range(NF):
                        P.add("pe", lambda eng, c=c, j=j, vc=vc: eng.matmul(
                            PS[vc][:, 0:512], BIGA[:, j, c * 128:(c + 1) * 128],
                            WQ[:, j, 2048 + vc * 512:2048 + (vc + 1) * 512],
                            start=(j == 0), stop=(j == NF - 1)),
                            reads=(f"WQ{j}", castk[j]), writes=(bank,))
                    P.add("dve",
                          lambda eng, vc=vc, tg=tg: copy_to(eng,
                              VR[:, tg % 6, vc * 512:(vc + 1) * 512], PS[vc][:, 0:512]),
                          reads=(bank,), writes=(f"vr{tg % 6}",))

            for c in range(4):
                tg = s * 4 + c
                kind = min(tg, 2)
                blocks = [(bi, tg - 2 + bi) for bi in range(3) if tg - 2 + bi >= 0]
                runs = []
                for bi, kb in blocks:
                    rc = kb % 8
                    if runs and runs[-1][1] + runs[-1][2] == rc:
                        runs[-1][2] += 1
                    else:
                        runs.append([bi, rc, 1])
                nvalid = sum(r[2] for r in runs)
                lo = runs[0][0] * 128
                hi = lo + nvalid * 128

                def sbank(w, hh):
                    i = (2 + hh) if w % 2 == 0 else hh
                    return PS[i], f"PS{i}"

                def smslot(w, hh):
                    buf = STATS if w % 2 == 0 else OUTB
                    return buf[:, hh * 384:(hh + 1) * 384], f"SMS{w % 2}{hh}"

                def emit_S(w):
                    for hh in range(2):
                        h = w * 2 + hh
                        g, pbs = h // 2, 64 * (h % 2)
                        Sb, bank = sbank(w, hh)
                        for ri, (bi, rc, ln) in enumerate(runs):
                            P.add("pe", lambda eng, Sb=Sb, g=g, pbs=pbs, c=c, bi=bi,
                                  rc=rc, ln=ln, first=(ri == 0),
                                  last=(ri == len(runs) - 1): eng.matmul(
                                Sb[:, bi * 128:(bi + ln) * 128],
                                QPR[pbs:pbs + 64, g, c * 128:(c + 1) * 128],
                                KR[pbs:pbs + 64, g, rc * 128:(rc + ln) * 128],
                                start=first, stop=last),
                                reads=(f"qp{g}",) + tuple(
                                    f"kp{g}_{((rc + x) // 4) % 2}" for x in range(ln)),
                                writes=(bank,))

                def emit_softmax_head(w, hh):
                    rr = w % 2
                    h = w * 2 + hh
                    Sb, bank = sbank(w, hh)
                    SMS, smk = smslot(w, hh)
                    P.add("dve", lambda eng, Sb=Sb, SMS=SMS, kind=kind, lo=lo, hi=hi:
                          eng.tensor_tensor(SMS[:, lo:hi], Sb[:, lo:hi],
                                            MSKF[:, kind, lo:hi], ALU.add),
                          reads=(bank, "MSKF"), writes=(smk,))
                    if lo > 0:
                        P.add("dve", lambda eng, SMS=SMS, kind=kind, lo=lo:
                              eng.tensor_copy(SMS[:, 0:lo], MSKF[:, kind, 0:lo]),
                              reads=("MSKF",), writes=(smk,))
                    P.add("act", lambda eng, SMS=SMS, rr=rr, hh=hh, c=c, h=h:
                          eng.activation(
                              PSBr[:, rr, hh, :], SMS[:, 0:384], AF.Exp,
                              scale=IVQC[:, c, h:h + 1],
                              accum_out=SUMS[:, h:h + 1]),
                          reads=(smk, f"ivqc{c}"),
                          writes=(f"psb{rr}{hh}", f"sums{h}"))

                def emit_pdiv(w, hh):
                    rr = w % 2
                    h = w * 2 + hh
                    P.add("dve", lambda eng, h=h: eng.reciprocal(
                        OSC[:, h:h + 1], SUMS[:, h:h + 1]),
                        reads=(f"sums{h}",), writes=(f"osc{h}",))
                    P.add("pool", lambda eng, rr=rr, hh=hh, h=h: eng.tensor_scalar(
                        PSBr[:, rr, hh, :], PSBr[:, rr, hh, :], OSC[:, h:h + 1],
                        None, ALU.mult),
                        reads=(f"osc{h}", f"psb{rr}{hh}"), writes=(f"psb{rr}{hh}",))

                def emit_tail(w):
                    rr = w % 2
                    pbk = f"PB{w % 2}"
                    for hh in range(2):
                        for cc in range(3):
                            P.add("pe", lambda eng, w=w, rr=rr, hh=hh, cc=cc: eng.transpose(
                                PB[w % 2][:, hh * 384 + cc * 128:hh * 384 + (cc + 1) * 128],
                                PSBr[:, rr, hh, cc * 128:(cc + 1) * 128], IDB[:]),
                                reads=(f"psb{rr}{hh}", "IDB"), writes=(pbk,))
                    P.add("dve",
                          lambda eng, w=w, rr=rr: copy_to(eng,
                              PTBr[:, rr, :, :].rearrange("p a b -> p (a b)"),
                              PB[w % 2][:, 0:768]),
                          reads=(pbk,), writes=(f"ptb{rr}0", f"ptb{rr}1"))
                    for hh in range(2):
                        h = w * 2 + hh
                        pr = h // 2
                        Ob = PS[4] if pr < 4 else PS[5]
                        obank = f"PS{4 if pr < 4 else 5}"
                        for ii, (bi, kb) in enumerate(blocks):
                            P.add("pe", lambda eng, Ob=Ob, h=h, pr=pr, bi=bi, kb=kb, hh=hh,
                                  rr=rr,
                                  first=(ii == 0), last=(ii == len(blocks) - 1): eng.matmul(
                                Ob[64 * (h % 2):64 * (h % 2) + 64,
                                   (pr % 4) * 128:(pr % 4) * 128 + 128],
                                VR[:, kb % 6, h * 64:(h + 1) * 64],
                                PTBr[:, rr, hh, bi * 128:(bi + 1) * 128],
                                start=first, stop=last),
                                reads=(f"vr{kb % 6}", f"ptb{rr}{hh}"), writes=(obank,))

                emit_S(0)
                for w in range(8):
                    emit_softmax_head(w, 0)
                    emit_softmax_head(w, 1)
                    emit_pdiv(w, 0)
                    emit_pdiv(w, 1)
                    if w < 7:
                        emit_S(w + 1)
                    emit_tail(w)
                for pr in range(8):
                    Ob = PS[4] if pr < 4 else PS[5]
                    obank = f"PS{4 if pr < 4 else 5}"
                    P.add("dve",
                          lambda eng, pr=pr, c=c, Ob=Ob: copy_to(eng,
                              OFMv[:, pr, (c % 2) * 128:(c % 2) * 128 + 128],
                              Ob[:, (pr % 4) * 128:(pr % 4) * 128 + 128]),
                          reads=(obank,), writes=(f"ofm{pr}",))
                if c % 2 == 1:
                    for m in range(NF):
                        wr = (s * 2 + c // 2) % 2 if False else ((s * 8 + (c // 2) * 4 + m) % 2)
                        dmaop("pool", WOB[:, wr, :].rearrange("p (j c) -> p j c", j=8),
                              D["Wo"][l, :, m * 128:(m + 1) * 128]
                              .rearrange("(j p) c -> p j c", p=128),
                              f"wob{wr}", rkeys=(f"wodone{wr}",))
                        bank = f"PS{m % 2}"
                        for j in range(NF):
                            P.add("pe", lambda eng, m=m, j=j, wr=wr: eng.matmul(
                                PS[m % 2][:, 0:256], WOB[:, wr, j * 128:(j + 1) * 128],
                                OFMv[:, j, :], start=(j == 0), stop=(j == NF - 1)),
                                reads=(f"wob{wr}",) + tuple(f"ofm{p2}" for p2 in range(8)),
                                writes=(bank,))
                        ccols = slice(1 + s * 512 + (c // 2) * 256, 1 + s * 512 + (c // 2) * 256 + 256)
                        P.add("dve", lambda eng, m=m, s=s, ccols=ccols, wr=wr: eng.tensor_tensor(
                            Rb[:, m, ccols], Rb[:, m, ccols], PS[m % 2][:, 0:256], ALU.add),
                            reads=(bank, f"R{m}.{s}"), writes=(f"R{m}.{s}", f"wodone{wr}"))
        P.barrier()

    def emit_ffn(l):
        for kc in range(NKC):
            kn = 128 if kc < NKC - 1 else 42
            dmaop("pool", W2R[0:kn, kc, :], D["W2"][l, kc * 128:kc * 128 + kn, :], f"w2{kc}")
        dmaop("pool", W2R[42:43, NKC - 1, :], D["b2"][l].rearrange("(a c) -> a c", a=1), "b2row")
        dmaop("pool", MH[42:43, NKC - 1, :], D["ones_512"], "onesm")
        dmaop("sp", OUTB[0:21, 0:128], D["b1"][l, 0:2688].rearrange("(a b) -> a b", a=21),
              ("b1ar", "OUTB"), rkeys=("PS0",))
        dmaop("sp", OUTB[21:22, 0:42], D["b1"][l, 2688:2730].rearrange("(a c) -> a c", a=1),
              ("b1ar2", "OUTB"), rkeys=("PS0",))
        P.add("pe", lambda eng: eng.transpose(PS[0][:, 0:22], OUTB[0:22, 0:128], IDFB[0:22, 0:22]),
              reads=("b1ar", "b1ar2", "IDF", "OUTB"), writes=("PS0",))
        P.add("dve", lambda eng: eng.tensor_copy(B1A[:, :], PS[0][:, 0:22]),
              reads=("PS0",), writes=("b1a",))
        dmaop("sp", OUTB[0:21, 0:128], D["b1"][l, 2730:5418].rearrange("(a b) -> a b", a=21),
              ("b1gr", "OUTB"), rkeys=("PS0",))
        dmaop("sp", OUTB[21:22, 0:42], D["b1"][l, 5418:5460].rearrange("(a c) -> a c", a=1),
              ("b1gr2", "OUTB"), rkeys=("PS0",))
        P.add("pe", lambda eng: eng.transpose(PS[0][:, 0:22], OUTB[0:22, 0:128], IDFB[0:22, 0:22]),
              reads=("b1gr", "b1gr2", "IDF", "OUTB"), writes=("PS0",))
        P.add("dve", lambda eng: eng.tensor_copy(B1G[:, :], PS[0][:, 0:22]),
              reads=("PS0",), writes=("b1g",))
        for nm, coff, key in (("lng", 0, "lngc"), ("lnb", 8, "lnbc")):
            dmaop("sp", OUTB[0:8, 0:128], D[nm][l].rearrange("(a b) -> a b", a=8),
                  (f"{nm}r", "OUTB"), rkeys=("PS0",))
            P.add("pe", lambda eng, nm=nm: eng.transpose(
                PS[0][:, 0:8], OUTB[0:8, 0:128], IDFB[0:8, 0:8]),
                reads=(f"{nm}r", "IDF", "OUTB"), writes=("PS0",))
            P.add("dve", lambda eng, coff=coff: eng.tensor_copy(
                LNGB[:, coff:coff + 8], PS[0][:, 0:8]),
                reads=("PS0",), writes=(key,))

        for hf in (1, 0):
            for si in range(2):
                s = hf * 2 + si
                emit_ln_stats_rows(s, shifted=True, rk=f"fl{s}")
                for f in range(NF):
                    P.add("dve", lambda eng, f=f, s=s, si=si: eng.tensor_tensor(
                        SCR[:, si, :], Rb[:, f, shifted_cols(f, s)], PS[4][:, 0:512],
                        ALU.subtract),
                        reads=("PS4",) + rk_shift(f, s), writes=(f"SCR{si}",))
                    P.add("dve", lambda eng, f=f, si=si: eng.tensor_tensor(
                        SCR[:, si, :], SCR[:, si, :], PS[5][:, 0:512], ALU.mult),
                        reads=("PS5",), writes=(f"SCR{si}",))
                    P.add("pool", lambda eng, f=f, si=si: eng.tensor_scalar(
                        XNH[:, f, si * 512:(si + 1) * 512], SCR[:, si, :],
                        LNGB[:, f:f + 1], LNGB[:, 8 + f:9 + f], ALU.mult, ALU.add),
                        reads=(f"SCR{si}", "lngc", "lnbc"), writes=(f"xn{f}",))
            for kc in range(NKC):
                kn = 128 if kc < NKC - 1 else 42
                wr = kc % 2
                for ab in range(2):
                    cbase = ab * DF + kc * 128
                    dmaop("pool", W1R[:, wr, ab, :, 0:kn],
                          D["W1"][l, :, cbase:cbase + kn].rearrange("(j p) c -> p j c", p=128),
                          f"w1{wr}{ab}", rkeys=(f"w1d{wr}{ab}",))
                for si in range(2):
                    bb = ((kc * 2 + si) % 2) * 2
                    for ab in range(2):
                        bank = f"PS{bb + ab}"
                        for j in range(NF):
                            P.add("pe", lambda eng, wr=wr, ab=ab, j=j, si=si, kn=kn,
                                  bb=bb: eng.matmul(
                                PS[bb + ab][0:kn, 0:512], W1R[:, wr, ab, j, 0:kn],
                                XNH[:, j, si * 512:(si + 1) * 512],
                                start=(j == 0), stop=(j == NF - 1)),
                                reads=(f"w1{wr}{ab}", f"xn{j}"), writes=(bank,))
                    P.add("dve", lambda eng, kc=kc, si=si, kn=kn, bb=bb: eng.tensor_scalar(
                        SCR[0:kn, si, :], PS[bb + 1][0:kn, 0:512],
                        B1G[0:kn, kc:kc + 1], None, ALU.add),
                        reads=(f"PS{bb + 1}", "b1g"), writes=(f"SCR{si}",))
                    P.add("act", lambda eng, kc=kc, si=si, kn=kn: eng.activation(
                        BIGA[0:kn, si, 0:512], SCR[0:kn, si, :], AF.Gelu),
                        reads=(f"SCR{si}",), writes=(f"gel{si}",))
                    wx = (f"w1d{wr}0", f"w1d{wr}1") if si == 1 else ()
                    P.add("dve", lambda eng, kc=kc, si=si, hf=hf, kn=kn, bb=bb:
                          eng.scalar_tensor_tensor(
                              MH[0:kn, kc, si * 512:(si + 1) * 512], PS[bb][0:kn, 0:512],
                              B1A[0:kn, kc:kc + 1], BIGA[0:kn, si, 0:512],
                              ALU.add, ALU.mult),
                          reads=(f"PS{bb}", "b1a", f"gel{si}"),
                          writes=(f"m{kc}",) + wx)
            for si in range(2):
                s = hf * 2 + si
                for m in range(NF):
                    bank = f"PS{m % 2}"
                    for kc in range(NKC):
                        kn = 128 if kc < NKC - 1 else 43
                        P.add("pe", lambda eng, m=m, kc=kc, si=si, kn=kn: eng.matmul(
                            PS[m % 2][:, 0:512], W2R[0:kn, kc, m * 128:(m + 1) * 128],
                            MH[0:kn, kc, si * 512:(si + 1) * 512],
                            start=(kc == 0), stop=(kc == NKC - 1)),
                            reads=(f"w2{kc}", f"m{kc}") +
                                  (("b2row", "onesm") if kc == NKC - 1 else ()),
                            writes=(bank,))
                    P.add("dve", lambda eng, m=m, s=s: eng.tensor_tensor(
                        Rb[:, m, SL(s)], Rb[:, m, SL(s)], PS[m % 2][:, 0:512], ALU.add),
                        reads=(bank, f"R{m}.{s}"), writes=(f"R{m}.{s}",))
        P.barrier()

    for l in range(layers):
        emit_attn(l)
        if not skip_ffn:
            emit_ffn(l)

    if add_ssm:
        for j in range(NT):
            s4 = j // 4
            P.add("sp", lambda eng, j=j: eng.dma_start(
                out=OUTB[:].rearrange("p (f c) -> p f c", f=8),
                in_=ssm_scr[:, :, j * 128:(j + 1) * 128].rearrange("f p c -> p f c")),
                reads=("fssc",), writes=("fss", "OUTB"), dma=True)
            for f in range(NF):
                P.add("dve", lambda eng, f=f, j=j: eng.tensor_tensor(
                    Rb[:, f, 1 + j * 128:1 + (j + 1) * 128],
                    Rb[:, f, 1 + j * 128:1 + (j + 1) * 128],
                    OUTB[:].rearrange("p (f c) -> p f c", f=8)[:, f, :], ALU.add),
                    reads=("fss", f"R{f}.{s4}", "OUTB"),
                    writes=(f"R{f}.{s4}",) + (("fssc",) if f == NF - 1 else ()))
    P.barrier()
    for j in range(0 if "final" in _SKIP else NT):
        s4 = j // 4
        if add_x:
            dmaop("sp", OUTB[:], D["x"][j * 128:(j + 1) * 128, :], ("fx", "OUTB"), rkeys=("fstored",))
        for fg in range(2):
            bank = f"PS{fg}"
            for ff in range(4):
                f = fg * 4 + ff
                P.add("pe", lambda eng, f=f, j=j, fg=fg, ff=ff, s4=s4: eng.transpose(
                    PS[fg][:, ff * 128:(ff + 1) * 128],
                    Rb[:, f, 1 + j * 128:1 + (j + 1) * 128], IDFB[:]),
                    reads=(f"R{f}.{s4}", "IDF"), writes=(bank,))
            if add_x:
                P.add("dve", lambda eng, fg=fg: eng.tensor_tensor(
                    OUTB[:, fg * 512:(fg + 1) * 512], PS[fg][:, 0:512],
                    OUTB[:, fg * 512:(fg + 1) * 512], ALU.add),
                    reads=(bank, "fx"), writes=(f"fo{fg}", "OUTB"))
            else:
                P.add("dve", lambda eng, fg=fg: eng.tensor_copy(
                    OUTB[:, fg * 512:(fg + 1) * 512], PS[fg][:, 0:512]),
                    reads=(bank,), writes=(f"fo{fg}", "OUTB"))
        P.add("sp", lambda eng, j=j: eng.dma_start(
            out=OUT[j * 128:(j + 1) * 128, :], in_=OUTB[:]),
            reads=("fo0", "fo1", "OUTB"), writes=("fstored",), dma=True, force_inc=True)

    P.emit(block, sems)
    nc._keepalive_ctx = ctx
    return nc


def kernel(**inputs):
    cfg = (_CFG["layers"], _CFG["add_ssm"], _CFG["add_x"], _CFG.get("skip_ffn", False))
    if cfg not in _NC_CACHE:
        _NC_CACHE[cfg] = build(cfg)
    nc = _NC_CACHE[cfg]
    C = _host_consts()

    shared = {}
    shared["Win"] = inputs["L0_attn_Win"]
    shared["bin"] = inputs["L0_attn_bin_"]
    shared["Bp"] = inputs["L0_attn_Bp"]
    shared["Cp"] = inputs["L0_attn_Cp"]
    shared["Dp"] = inputs["L0_attn_Dp"]
    shared["Wout"] = inputs["L0_attn_Wout"]
    shared["bout"] = inputs["L0_attn_bout"]
    shared["Wqkv"] = inputs["L0_ffn_Wqkv"]
    shared["qsc"] = inputs["L0_ffn_q_scale"]
    shared["ksc"] = inputs["L0_ffn_k_scale"]
    shared["Wo"] = inputs["L0_ffn_Wo"]
    shared["lng"] = inputs["L0_ffn_ln_g"]
    shared["lnb"] = inputs["L0_ffn_ln_b"]
    shared["W1"] = inputs["L0_ffn_W1"]
    shared["b1"] = inputs["L0_ffn_b1"]
    shared["W2"] = inputs["L0_ffn_W2"]
    shared["b2"] = inputs["L0_ffn_b2"]
    for k in ("tab_cos", "tab_sin", "tab_scl", "tab_isc", "R2", "IDF", "MSK", "BD",
              "SEL", "T_hi", "T_lo", "dinj_hi", "dinj_lo", "dcol_hi", "dcol_lo",
              "p128_hi", "p128_lo", "ones_row128", "ones_col", "ones_512"):
        shared[k] = C[k]
    shared = {k: np.ascontiguousarray(np.asarray(v, dtype=np.float32))
              for k, v in shared.items()}

    x = np.asarray(inputs["L0_attn_x"], dtype=np.float32)
    in_maps = []
    for c in range(B):
        m = dict(shared)
        m["x"] = np.ascontiguousarray(x[c])
        in_maps.append(m)

    res = run_bass_kernel_spmd(nc, in_maps, core_ids=list(range(B)))
    return np.stack([np.asarray(r["out"]) for r in res.results], axis=0)


# version 04
# speedup vs baseline: 1.6252x; 1.6252x over previous
import numpy as np
from contextlib import ExitStack

import concourse.bass as bass
import concourse.mybir as mybir
from concourse.bass_utils import run_bass_kernel_spmd

F32 = mybir.dt.float32
BF16 = mybir.dt.bfloat16
AF = mybir.ActivationFunctionType
ALU = mybir.AluOpType
AX = mybir.AxisListType

B = 8
T = 2048
DM = 1024
NF = 8
DI = 2048
NDI = 16
DH = 64
WIN = 256
DF = 2730
NKC = 22
DEPTH = 6
NT = 16
NS = 4
SS = 512
NEG = -1.0e30

_CFG = dict(layers=DEPTH, add_ssm=True, add_x=True, skip_ffn=False)
_SKIP = set()
_NC_CACHE = {}
_CONST_CACHE = {}


def _host_consts():
    if _CONST_CACHE:
        return _CONST_CACHE
    import ml_dtypes
    try:
        import jax
        import jax.numpy as jnp
        _cpu = jax.devices("cpu")[0]
        with jax.default_device(_cpu):
            half = jnp.arange(0, DH, 2, dtype=jnp.float32)
            inv_freq = 1.0 / 10000.0 ** (half / DH)
            t = jnp.arange(T, dtype=jnp.float32)
            freqs = t[:, None] * inv_freq[None, :]
            freqs = jnp.concatenate([freqs, freqs], -1)
            base_scale = (half + 0.4 * DH) / (1.4 * DH)
            power = (t - T // 2) / (WIN // 2)
            scale = base_scale[None, :] ** power[:, None]
            scale = jnp.concatenate([scale, scale], -1)
            cos = np.asarray(jnp.cos(freqs), dtype=np.float32)
            sin = np.asarray(jnp.sin(freqs), dtype=np.float32)
            scl = np.asarray(scale, dtype=np.float32)
            isc = np.asarray(1.0 / jnp.asarray(scale), dtype=np.float32)
    except Exception:
        half = np.arange(0, DH, 2, dtype=np.float32)
        inv_freq = (1.0 / (np.float32(10000.0) ** (half / np.float32(DH)))).astype(np.float32)
        t = np.arange(T, dtype=np.float32)
        freqs = np.concatenate([t[:, None] * inv_freq] * 2, -1).astype(np.float32)
        base_scale = ((half + np.float32(0.4 * DH)) / np.float32(1.4 * DH)).astype(np.float32)
        power = ((t - T // 2) / np.float32(WIN // 2)).astype(np.float32)
        scl = np.concatenate([(base_scale[None, :] ** power[:, None])] * 2, -1).astype(np.float32)
        cos = np.cos(freqs).astype(np.float32)
        sin = np.sin(freqs).astype(np.float32)
        isc = (np.float32(1.0) / scl).astype(np.float32)

    def dup128(a):
        return np.ascontiguousarray(np.concatenate([a.T, a.T], axis=0).astype(np.float32))

    C = dict(tab_cos=dup128(cos), tab_sin=dup128(sin), tab_scl=dup128(scl), tab_isc=dup128(isc))

    R2 = np.zeros((128, 128), dtype=np.float32)
    for base in (0, 64):
        for d in range(32):
            R2[base + d + 32, base + d] = -1.0
            R2[base + d, base + d + 32] = 1.0
    C["R2"] = R2
    C["IDF"] = np.eye(128, dtype=np.float32)

    msk = np.zeros((3, 128, 384), dtype=np.float32)
    ii = np.arange(128)[:, None]
    jj = np.arange(384)[None, :]
    k0 = jj - 256
    msk[0][(k0 < 0) | (k0 > ii)] = NEG
    k1 = jj - 128
    msk[1][(k1 < 0) | (k1 > 128 + ii)] = NEG
    msk[2][(jj < ii) | (jj > ii + 256)] = NEG
    C["MSK"] = msk

    bd = np.zeros((8, 128, 16), dtype=np.float32)
    for g in range(8):
        bd[g, 0:64, 2 * g] = 1.0
        bd[g, 64:128, 2 * g + 1] = 1.0
    C["BD"] = bd
    sel = np.zeros((8, 16, 128), dtype=np.float32)
    for p in range(8):
        sel[p, 2 * p, 0:64] = 1.0
        sel[p, 2 * p + 1, 64:128] = 1.0
    C["SEL"] = sel

    def hilo(x):
        hi = x.astype(ml_dtypes.bfloat16).astype(np.float64)
        lo = (x - hi).astype(ml_dtypes.bfloat16).astype(np.float64)
        return hi.astype(np.float32), lo.astype(np.float32)

    d9 = np.float64(0.9)
    Tt = np.zeros((128, 128), dtype=np.float64)
    for s in range(128):
        Tt[s, s:] = d9 ** np.arange(128 - s)
    C["T_hi"], C["T_lo"] = hilo(Tt)
    C["dinj_hi"], C["dinj_lo"] = hilo((d9 ** np.arange(1, 129)).reshape(1, 128))
    C["dcol_hi"], C["dcol_lo"] = hilo((d9 ** (127 - np.arange(128))).reshape(128, 1))
    C["p128_hi"], C["p128_lo"] = hilo(np.array([[d9 ** 128]]))
    C["ones_row128"] = np.ones((1, 128), dtype=np.float32)
    C["ones_col"] = np.ones((128, 1), dtype=np.float32)
    C["ones_512"] = np.ones((1, 1024), dtype=np.float32)
    _CONST_CACHE.update(C)
    return _CONST_CACHE


ENGINES = ("pe", "act", "dve", "pool", "sp")


class Prog:
    def __init__(self):
        self.ops = []
        self.last_writer = {}
        self.readers = {}
        self.barrier_deps = []

    def add(self, engine, fn, reads=(), writes=(), dma=False, force_inc=False):
        deps = set(self.barrier_deps)
        for k in reads:
            w = self.last_writer.get(k)
            if w is not None:
                deps.add(w)
        for k in writes:
            w = self.last_writer.get(k)
            if w is not None:
                deps.add(w)
            deps.update(self.readers.get(k, ()))
        i = len(self.ops)
        self.ops.append(dict(engine=engine, fn=fn, deps=deps,
                             inc=16 if dma else 1, force=force_inc or dma))
        for k in reads:
            self.readers.setdefault(k, []).append(i)
        for k in writes:
            self.last_writer[k] = i
            self.readers[k] = []
        return i

    def barrier(self):
        last = {}
        for i, op in enumerate(self.ops):
            last[op["engine"]] = i
        self.barrier_deps = list(last.values())

    def emit(self, block, sems):
        NQ = 8
        n = len(self.ops)
        fdeps = []
        for i, op in enumerate(self.ops):
            ds = set()
            for d in op["deps"]:
                if self.ops[d]["engine"] == "pe" and op["engine"] == "pe":
                    continue
                ds.add(d)
            fdeps.append(ds)
        needed = [op["force"] for op in self.ops]
        for i in range(n):
            for d in fdeps[i]:
                needed[d] = True
        sem_of = [None] * n
        counters = {e: 0 for e in ENGINES}
        qidx = {e: 0 for e in ENGINES}
        pre_wait = [None] * n
        for i, op in enumerate(self.ops):
            e = op["engine"]
            if op["inc"] == 16:
                j = qidx[e]
                qidx[e] = j + 1
                slot = f"{e}q{j % NQ}"
                rnd = j // NQ
                if rnd > 0:
                    pre_wait[i] = (slot, 16 * rnd)
                sem_of[i] = (slot, 16 * (rnd + 1))
            elif needed[i]:
                counters[e] += 1
                sem_of[i] = (e, counters[e])
        self.totals = dict(counters)
        self.qtotals = dict(qidx)

        def make_body(ename):
            my_ops = [(i, op) for i, op in enumerate(self.ops) if op["engine"] == ename]

            def body(eng):
                hw = {}
                def wait(sname, v):
                    if v > hw.get(sname, 0):
                        eng.wait_ge(sems[sname], v)
                        hw[sname] = v
                for i, op in my_ops:
                    waits = {}
                    for d in fdeps[i]:
                        sname, v = sem_of[d]
                        if v > waits.get(sname, 0):
                            waits[sname] = v
                    if pre_wait[i] is not None:
                        sname, v = pre_wait[i]
                        if v > waits.get(sname, 0):
                            waits[sname] = v
                    for sname in sorted(waits):
                        wait(sname, waits[sname])
                    inst = op["fn"](eng)
                    if sem_of[i] is not None:
                        sname, v = sem_of[i]
                        inst.then_inc(sems[sname], 16 if op["inc"] == 16 else 1)
                if ename == "sp":
                    for j in range(min(NQ, self.qtotals.get("sp", 0))):
                        tot = self.qtotals["sp"]
                        full = 16 * ((tot - 1 - j) // NQ + 1) if j < tot else 0
                        wait(f"spq{j}", full)
                    for j in range(min(NQ, self.qtotals.get("pool", 0))):
                        tot = self.qtotals["pool"]
                        full = 16 * ((tot - 1 - j) // NQ + 1) if j < tot else 0
                        wait(f"poolq{j}", full)
            return body

        block.tensor(make_body("pe"))
        block.scalar(make_body("act"))
        block.vector(make_body("dve"))
        block.gpsimd(make_body("pool"))
        block.sync(make_body("sp"))


def copy_to(eng, out, in_):
    if hasattr(eng, "tensor_copy"):
        return eng.tensor_copy(out, in_)
    return eng.copy(out, in_)


def carve(arena, off, shape):
    n = int(np.prod(shape))
    ap_ = arena[:, off:off + n]
    if len(shape) > 1:
        names = "abcd"[: len(shape)]
        spec = "p (" + " ".join(names) + ") -> p " + " ".join(names)
        kw = {names[i]: shape[i] for i in range(len(shape))}
        ap_ = ap_.rearrange(spec, **kw)
    return ap_, off + n


def build(cfg_key):
    layers, add_ssm, add_x, skip_ffn = cfg_key
    nc = bass.Bass("TRN2", target_bir_lowering=False, debug=False)

    D = {}
    def din(name, shape):
        D[name] = nc.dram_tensor(name, list(shape), F32, kind="ExternalInput").ap()
    din("x", (T, DM))
    din("Win", (DM, DI)); din("bin", (DI,)); din("Bp", (DI, 16)); din("Cp", (16, DI))
    din("Dp", (DI,)); din("Wout", (DI, DM)); din("bout", (DM,))
    din("Wqkv", (DEPTH, DM, 3 * DM)); din("qsc", (DEPTH, DH)); din("ksc", (DEPTH, DH))
    din("Wo", (DEPTH, DM, DM)); din("lng", (DEPTH, DM)); din("lnb", (DEPTH, DM))
    din("W1", (DEPTH, DM, 2 * DF)); din("b1", (DEPTH, 2 * DF))
    din("W2", (DEPTH, DF, DM)); din("b2", (DEPTH, DM))
    for k in ("tab_cos", "tab_sin", "tab_scl", "tab_isc"):
        din(k, (128, T))
    din("R2", (128, 128)); din("IDF", (128, 128)); din("MSK", (3, 128, 384))
    din("BD", (8, 128, 16)); din("SEL", (8, 16, 128))
    din("T_hi", (128, 128)); din("T_lo", (128, 128))
    din("dinj_hi", (1, 128)); din("dinj_lo", (1, 128))
    din("dcol_hi", (128, 1)); din("dcol_lo", (128, 1))
    din("p128_hi", (1, 1)); din("p128_lo", (1, 1))
    din("ones_row128", (1, 128)); din("ones_col", (128, 1)); din("ones_512", (1, 1024))

    ssm_scr = nc.dram_tensor("ssm_scr", [NF, 128, T], F32, kind="Internal").ap()
    tab_scr = nc.dram_tensor("tab_scr", [4, 128, T], BF16, kind="Internal").ap()
    OUT = nc.dram_tensor("out", [T, DM], F32, kind="ExternalOutput").ap()

    ctx = ExitStack()
    E = ctx.enter_context

    Rb = E(nc.sbuf_tensor("Rb", [128, NF, 2049], F32))
    AA = E(nc.sbuf_tensor("AA", [128, 24576], BF16))
    KR = E(nc.sbuf_tensor("KR", [128, NF, 1024], BF16))
    ARX = E(nc.sbuf_tensor("ARX", [128, 22528], BF16))
    BIGA = E(nc.sbuf_tensor("BIGA", [128, NF, 512], BF16))
    WR = E(nc.sbuf_tensor("WR", [128, 4096], BF16))
    OUTB = E(nc.sbuf_tensor("OUTB", [128, 1024], F32))
    SCR = E(nc.sbuf_tensor("SCR", [128, 2, 512], BF16))
    STATS = E(nc.sbuf_tensor("STATS", [128, 1024], F32))
    INV = E(nc.sbuf_tensor("INV", [64, 512], F32))
    IVQC = E(nc.sbuf_tensor("IVQC", [128, 4, 16], F32))
    NMX = E(nc.sbuf_tensor("NMX", [128, 8], F32))
    SUMS = E(nc.sbuf_tensor("SUMS", [128, 16], F32))
    OSC = E(nc.sbuf_tensor("OSC", [128, 16], F32))
    QKCOL = E(nc.sbuf_tensor("QKCOL", [128, 2], F32))
    BCOL = E(nc.sbuf_tensor("BCOL", [128, 64], F32))
    ETMP = STATS[:, 768:784]
    B1A = BCOL[:, 0:22]
    B1G = BCOL[:, 22:44]
    LNGB = BCOL[:, 44:60]
    R2B = E(nc.sbuf_tensor("R2B", [128, 128], BF16))
    IDFB = E(nc.sbuf_tensor("IDFB", [128, 128], F32))
    IDB = E(nc.sbuf_tensor("IDB", [128, 128], BF16))
    MSKF = E(nc.sbuf_tensor("MSKF", [128, 3, 384], BF16))
    BDB = E(nc.sbuf_tensor("BDB", [128, 8, 16], BF16))
    SELB = E(nc.sbuf_tensor("SELB", [48, 8, 128], BF16))
    THI = E(nc.sbuf_tensor("THI", [128, 128], BF16))
    TLO = E(nc.sbuf_tensor("TLO", [128, 128], BF16))
    DINJB = E(nc.sbuf_tensor("DINJB", [1, 256], BF16))
    DINJ = DINJB[:, 0:128]
    DINJ2 = DINJB[:, 128:256]
    DCOL = E(nc.sbuf_tensor("DCOL", [128, 2], BF16))
    P128 = E(nc.sbuf_tensor("P128", [1, 2], BF16))
    ONER = E(nc.sbuf_tensor("ONER", [1, 128], F32))
    ONECB = E(nc.sbuf_tensor("ONECB", [128, 1], BF16))

    rem = nc.sbuf_bytes_remaining
    assert rem >= 0, f"SBUF over budget by {-rem} bytes"

    off = 0
    VR, off = carve(ARX, off, (6, 1024))
    TABR, off = carve(ARX, off, (4, 512))
    QPR, off = carve(ARX, off, (8, 512))
    QSR, off = carve(ARX, off, (2, 512))
    KSR, off = carve(ARX, off, (2, 512))
    T2R, off = carve(ARX, off, (2, 512))
    IVB, off = carve(ARX, off, (2, 512))
    PSBr, off = carve(ARX, off, (2, 2, 384))
    PTBr, off = carve(ARX, off, (2, 2, 384))
    OFMv, off = carve(ARX, off, (8, 256))
    assert off <= 22528, off
    W2R, _o = carve(ARX, 0, (NKC, 1024))
    assert _o <= 22528
    off = 0
    XIF, off = carve(ARX, off, (16, 256))
    XIT, off = carve(ARX, off, (2, 2048))
    YF, off = carve(ARX, off, (16, 256))
    CEb, off = carve(ARX, off, (16, 256))
    assert off <= 22528

    WQ = AA.rearrange("p (a b) -> p a b", a=NF)
    MH = AA[:, 0:NKC * 1024].rearrange("p (k c) -> p k c", k=NKC)
    WINV = AA[:, 0:16384].rearrange("p (a b) -> p a b", a=NF)
    WOUTA = AA[:, 16384:24576].rearrange("p (a b) -> p a b", a=8)
    WOUTB = KR.rearrange("p a b -> p (a b)").rearrange("p (a b) -> p a b", a=8)
    XNH = KR
    WOB = WR[:, 0:2048].rearrange("p (r c) -> p r c", r=2)
    R2QS = WR[:, 2048:2176]
    R2KS = WR[:, 2176:2304]
    W1R = WR.rearrange("p (r a j c) -> p r a j c", r=2, a=2, j=8)
    CPST = OUTB[0:16, 0:128]
    BPST = OUTB[:, 128:144]

    PS = [E(nc.psum_tensor(f"PS{i}", [128, 512], F32)) for i in range(6)]
    PB = [E(nc.psum_tensor(f"PBb{i}", [128, 768], BF16)) for i in range(2)]

    sems = {e: E(nc.semaphore(f"sem_{e}")) for e in ENGINES}
    for e in ("sp", "pool"):
        for j in range(8):
            sems[f"{e}q{j}"] = E(nc.semaphore(f"sem_{e}q{j}"))
    block = E(nc.Block())
    P = Prog()

    def dmaop(q, dst, src, wkey, rkeys=(), force=False):
        wk = (wkey,) if isinstance(wkey, str) else tuple(wkey)
        P.add(q, lambda eng, d=dst, s=src: eng.dma_start(out=d, in_=s),
              reads=rkeys, writes=wk, dma=True, force_inc=force)

    if "consts" in _SKIP:
        def dmaop_consts(*a, **k):
            pass
    else:
        dmaop_consts = dmaop
    dmaop = dmaop if "consts" not in _SKIP else dmaop_consts
    _real_dmaop = dmaop
    dmaop_c = dmaop_consts
    dmaop_c("pool", R2B[:], D["R2"], "R2B")
    dmaop_c("pool", IDB[:], D["IDF"], "IDB")
    dmaop_c("sp", IDFB[:], D["IDF"], "IDF")
    dmaop_c("pool", MSKF[:], D["MSK"].rearrange("k p n -> p k n"), "MSKF")
    dmaop_c("pool", BDB[:], D["BD"].rearrange("k p n -> p k n"), "BDB")
    dmaop_c("pool", SELB[32:48, :, :], D["SEL"].rearrange("k p n -> p k n"), "SELB")
    dmaop_c("pool", THI[:], D["T_hi"], "THI")
    dmaop_c("pool", TLO[:], D["T_lo"], "TLO")
    dmaop_c("pool", DINJ[:], D["dinj_hi"], "DINJ")
    dmaop_c("pool", DINJ2[:], D["dinj_lo"], "DINJ2")
    dmaop_c("pool", DCOL[:, 0:1], D["dcol_hi"], "DCOL")
    dmaop_c("pool", DCOL[:, 1:2], D["dcol_lo"], "DCOLb")
    dmaop_c("pool", P128[0:1, 0:1], D["p128_hi"], "P128")
    dmaop_c("pool", P128[0:1, 1:2], D["p128_lo"], "P128b")
    dmaop_c("sp", ONER[:], D["ones_row128"], "ONER")
    dmaop_c("pool", ONECB[:], D["ones_col"], "ONECB")

    for idx, (a_name, b_name) in enumerate(() if "tab" in _SKIP else
                                           (("tab_cos", "tab_scl"), ("tab_sin", "tab_scl"),
                                            ("tab_cos", "tab_isc"), ("tab_sin", "tab_isc"))):
        for q4 in range(4):
            cols = slice(q4 * 512, (q4 + 1) * 512)
            sl = q4 % 2
            dmaop("sp", OUTB[:, 0:512], D[a_name][:, cols], ("tca", "OUTB"), rkeys=("tcp",))
            dmaop("sp", OUTB[:, 512:1024], D[b_name][:, cols], ("tcb", "OUTB"), rkeys=("tcp",))
            P.add("dve", lambda eng, sl=sl: eng.tensor_tensor(
                SCR[:, sl, :], OUTB[:, 0:512], OUTB[:, 512:1024], ALU.mult),
                reads=("tca", "tcb", "OUTB"), writes=("tcp", f"SCR{sl}"))
            P.add("sp", lambda eng, idx=idx, c=cols, sl=sl: eng.dma_start(
                out=tab_scr[idx, :, c], in_=SCR[:, sl, :]),
                reads=(f"SCR{sl}",), writes=(f"tdone{idx}{q4}",), dma=True)
    P.barrier()

    P.add("dve", lambda eng: eng.memset(Rb[:, :, 0:1], 0.0), writes=("Rz",))
    for j in range(0 if "xload" in _SKIP else NT):
        dmaop("sp", OUTB[:], D["x"][j * 128:(j + 1) * 128, :], ("xld", "OUTB"), rkeys=("xcons",))
        for fg in range(2):
            bank = f"PS{fg}"
            for ff in range(4):
                f = fg * 4 + ff
                P.add("pe", lambda eng, f=f, fg=fg, ff=ff: eng.transpose(
                    PS[fg][:, ff * 128:(ff + 1) * 128],
                    OUTB[:, f * 128:(f + 1) * 128], IDFB[:]),
                    reads=("xld", "IDF", "OUTB"), writes=(bank,))
            for ff in range(4):
                f = fg * 4 + ff
                def cp(eng, f=f, j=j, fg=fg, ff=ff):
                    return copy_to(eng, Rb[:, f, 1 + j * 128:1 + (j + 1) * 128],
                                   PS[fg][:, ff * 128:(ff + 1) * 128])
                wk = [f"R{f}.{j // 4}"]
                if fg == 1 and ff == 3:
                    wk.append("xcons")
                P.add("dve", cp, reads=(bank,), writes=tuple(wk))
    P.barrier()

    SL = lambda s: slice(1 + s * SS, 1 + (s + 1) * SS)

    def shifted_cols(f, s, n=SS):
        o = (0 if f >= 4 else 1) + s * n
        return slice(o, o + n)

    def rk_shift(f, s):
        if f >= 4 and s > 0:
            return (f"R{f}.{s}", f"R{f}.{s - 1}")
        return (f"R{f}.{s}",)


    def emit_ln_stats_rows(s, shifted, rk):
        for f in range(NF):
            fr = f % 2
            cols = shifted_cols(f, s) if shifted else SL(s)
            P.add("dve", lambda eng, f=f, cols=cols, fr=fr: eng.tensor_copy(
                SCR[:, fr, :], Rb[:, f, cols]),
                reads=(rk_shift(f, s) if shifted else (f"R{f}.{s}",)), writes=(f"SCR{fr}",))
            P.add("pe", lambda eng, f=f, fr=fr: eng.matmul(
                PS[5][0:1, 0:512], ONECB[:], SCR[:, fr, :],
                start=(f == 0), stop=(f == NF - 1)),
                reads=(f"SCR{fr}", "ONECB"), writes=("PS5",))
        for f in range(NF):
            fr = f % 2
            cols = shifted_cols(f, s) if shifted else SL(s)
            P.add("act", lambda eng, f=f, cols=cols, fr=fr: eng.activation(
                SCR[:, fr, :], Rb[:, f, cols], AF.Square),
                reads=(rk_shift(f, s) if shifted else (f"R{f}.{s}",)), writes=(f"SCR{fr}",))
            P.add("pe", lambda eng, f=f, fr=fr: eng.matmul(
                PS[4][0:1, 0:512], ONECB[:], SCR[:, fr, :],
                start=(f == 0), stop=(f == NF - 1)),
                reads=(f"SCR{fr}", "ONECB"), writes=("PS4",))
        P.add("dve", lambda eng: eng.tensor_scalar(
            STATS[0:1, 0:512], PS[5][0:1, 0:512], 1.0 / DM, None, ALU.mult),
            reads=("PS5",), writes=("ST0",))
        P.add("dve", lambda eng: eng.tensor_scalar(
            STATS[0:1, 512:1024], PS[4][0:1, 0:512], 1.0 / DM, None, ALU.mult),
            reads=("PS4",), writes=("ST0b",))
        P.add("dve", lambda eng: eng.tensor_tensor(
            INV[0:1, 0:512], STATS[0:1, 0:512], STATS[0:1, 0:512], ALU.mult),
            reads=("ST0",), writes=("IV0",))
        P.add("dve", lambda eng: eng.tensor_tensor(
            STATS[0:1, 512:1024], STATS[0:1, 512:1024], INV[0:1, 0:512], ALU.subtract),
            reads=("IV0",), writes=("ST0b",))
        P.add("dve", lambda eng: eng.tensor_scalar(
            STATS[0:1, 512:1024], STATS[0:1, 512:1024], 1e-5, None, ALU.add),
            reads=(), writes=("ST0b",))
        P.add("act", lambda eng: eng.activation(
            INV[0:1, 0:512], STATS[0:1, 512:1024], AF.Ln),
            reads=("ST0b",), writes=("IV0",))
        P.add("act", lambda eng: eng.activation(
            STATS[0:1, 512:1024], INV[0:1, 0:512], AF.Exp, scale=np.float32(-0.5)),
            reads=("IV0",), writes=("ST0b",))
        P.add("pe", lambda eng: eng.matmul(
            PS[4][:, 0:512], ONER[:], STATS[0:1, 0:512], start=True, stop=True),
            reads=("ST0", "ONER"), writes=("PS4",))
        P.add("pe", lambda eng: eng.matmul(
            PS[5][:, 0:512], ONER[:], STATS[0:1, 512:1024], start=True, stop=True),
            reads=("ST0b", "ONER"), writes=("PS5",))

    for s in range(() and 0 or (0 if "ln" in _SKIP else NS)):
        emit_ln_stats_rows(s, shifted=False, rk=f"ol{s}")
        for f in range(NF):
            P.add("dve", lambda eng, f=f, s=s: eng.tensor_tensor(
                Rb[:, f, SL(s)], Rb[:, f, SL(s)], PS[4][:, 0:512], ALU.subtract),
                reads=("PS4", f"R{f}.{s}"), writes=(f"R{f}.{s}",))
            P.add("dve", lambda eng, f=f, s=s: eng.tensor_tensor(
                Rb[:, f, SL(s)], Rb[:, f, SL(s)], PS[5][:, 0:512], ALU.mult),
                reads=("PS5", f"R{f}.{s}"), writes=(f"R{f}.{s}",))
    P.barrier()

    if add_ssm:
        for g in range(NF):
            dmaop("pool", WINV[:, g, :], D["Win"][g * 128:(g + 1) * 128, :], f"WIN{g}")
        for g in range(NDI):
            tgt = WOUTA[:, g, :] if g < 8 else WOUTB[:, g - 8, :]
            dmaop("pool", tgt, D["Wout"][g * 128:(g + 1) * 128, :], f"WOUT{g}")
        for nm, nrow, coff in (("bin", 16, 0), ("Dp", 16, 16), ("bout", 8, 32)):
            dmaop("sp", OUTB[0:nrow, 0:128], D[nm].rearrange("(a b) -> a b", a=nrow),
                  (f"rw{nm}", "OUTB"), rkeys=("PS0",))
            P.add("pe", lambda eng, nm=nm, nrow=nrow: eng.transpose(
                PS[0][:, 0:nrow], OUTB[0:nrow, 0:128], IDFB[0:nrow, 0:nrow]),
                reads=(f"rw{nm}", "IDF", "OUTB"), writes=("PS0",))
            P.add("dve", lambda eng, nrow=nrow, coff=coff: eng.tensor_copy(
                BCOL[:, coff:coff + nrow], PS[0][:, 0:nrow]),
                reads=("PS0",), writes=(f"col{coff}",))
        for g in range(NDI):
            dmaop("sp", CPST, D["Cp"][:, g * 128:(g + 1) * 128], ("cpst", "OUTB"), rkeys=("PS1",))
            dmaop("sp", BPST, D["Bp"][g * 128:(g + 1) * 128, :], ("bpst", "OUTB"), rkeys=("etmp",))
            P.add("pe", lambda eng: eng.transpose(
                PS[1][:, 0:16], CPST, IDFB[0:16, 0:16]),
                reads=("cpst", "IDF", "OUTB"), writes=("PS1",))
            P.add("dve", lambda eng: eng.tensor_tensor(
                ETMP[:, :], PS[1][:, 0:16], BPST, ALU.mult),
                reads=("PS1", "bpst", "OUTB"), writes=("etmp",))
            P.add("dve", lambda eng, g=g: eng.reduce_sum(
                BCOL[:, 40 + g:41 + g], ETMP[:, :], axis=AX.X),
                reads=("etmp",), writes=(f"ecol{g}",))
        P.add("dve", lambda eng: eng.memset(WR[0:1, :], 0.0), writes=("car",))
        P.barrier()

        SSS = 256
        for s in range(8):
            s4 = s // 2
            for f in range(NF):
                P.add("act", lambda eng, f=f, s=s: copy_to(eng,
                    BIGA[:, f, 0:SSS] if s % 2 == 0 else BIGA[:, f, 256:512],
                    Rb[:, f, slice(1 + s * SSS, 1 + (s + 1) * SSS)]),
                    reads=(f"R{f}.{s4}",), writes=(f"scst{f}",))
            bsl = (lambda f: BIGA[:, f, 0:SSS]) if s % 2 == 0 else (lambda f: BIGA[:, f, 256:512])
            for g in range(NDI):
                bank = f"PS{g % 2}"
                for j in range(NF):
                    P.add("pe", lambda eng, g=g, j=j, bsl=bsl: eng.matmul(
                        PS[g % 2][:, 0:SSS], WINV[:, j, g * 128:(g + 1) * 128],
                        bsl(j), start=(j == 0), stop=(j == NF - 1)),
                        reads=(f"WIN{j}", f"scst{j}"), writes=(bank,))
                P.add("dve", lambda eng, g=g: eng.tensor_scalar(
                    SCR[:, g % 2, 0:SSS], PS[g % 2][:, 0:SSS],
                    BCOL[:, g:g + 1], None, ALU.add),
                    reads=(bank, "col0"), writes=(f"SCR{g % 2}",))
                P.add("act", lambda eng, g=g: eng.activation(
                    XIF[:, g, :], SCR[:, g % 2, 0:SSS], AF.Silu),
                    reads=(f"SCR{g % 2}",), writes=(f"xif{g}",))
            for c in range(2):
                for w in range(3):
                    lo, hi = w * 6, min(NDI, w * 6 + 6)
                    bank = f"PB{w % 2}"
                    for gi, g in enumerate(range(lo, hi)):
                        P.add("pe", lambda eng, g=g, c=c, gi=gi, w=w: eng.transpose(
                            PB[w % 2][:, gi * 128:(gi + 1) * 128],
                            XIF[:, g, c * 128:(c + 1) * 128], IDB[:]),
                            reads=(f"xif{g}", "IDB"), writes=(bank,))
                    nn = hi - lo
                    P.add("dve", lambda eng, c=c, w=w, nn=nn, lo=lo: eng.tensor_copy(
                        XIT[:, c, lo * 128:(lo + nn) * 128], PB[w % 2][:, 0:nn * 128]),
                        reads=(bank,), writes=(f"xit{c}",))
            for c in range(2):
                cg = s * 2 + c
                cur, nxt = (cg % 2) * 2048, ((cg + 1) % 2) * 2048
                for g in range(NDI):
                    bank = f"PS{2 + g % 2}"
                    P.add("pe", lambda eng, g=g, c=c: eng.matmul(
                        PS[2 + g % 2][:, 0:128], XIT[:, c, g * 128:(g + 1) * 128], THI[:],
                        start=True, stop=False),
                        reads=(f"xit{c}", "THI"), writes=(bank,))
                    P.add("pe", lambda eng, g=g, c=c, last=(cg == 0): eng.matmul(
                        PS[2 + g % 2][:, 0:128], XIT[:, c, g * 128:(g + 1) * 128], TLO[:],
                        start=False, stop=last),
                        reads=(f"xit{c}", "TLO"), writes=(bank,))
                    if cg > 0:
                        crow = WR[0:1, cur:cur + 2048]
                        P.add("pe", lambda eng, g=g, crow=crow: eng.matmul(
                            PS[2 + g % 2][:, 0:128], crow[:, g * 128:(g + 1) * 128],
                            DINJ[:], start=False, stop=False),
                            reads=("car", "DINJ"), writes=(bank,))
                        P.add("pe", lambda eng, g=g, crow=crow: eng.matmul(
                            PS[2 + g % 2][:, 0:128], crow[:, g * 128:(g + 1) * 128],
                            DINJ2[:], start=False, stop=True),
                            reads=("car", "DINJ2"), writes=(bank,))
                    P.add("dve", lambda eng, g=g, c=c: eng.tensor_scalar(
                        CEb[:, g, c * 128:(c + 1) * 128], PS[2 + g % 2][:, 0:128],
                        BCOL[:, 40 + g:41 + g], None, ALU.mult),
                        reads=(bank, f"ecol{g}"), writes=(f"ce{g}",))
                    P.add("dve", lambda eng, g=g, c=c: eng.scalar_tensor_tensor(
                        YF[:, g, c * 128:(c + 1) * 128], XIF[:, g, c * 128:(c + 1) * 128],
                        BCOL[:, 16 + g:17 + g], CEb[:, g, c * 128:(c + 1) * 128],
                        ALU.mult, ALU.add),
                        reads=(f"xif{g}", "col16", f"ce{g}"), writes=(f"yf{g}",))
                cbanks = ((PS[5], "PS5"), (PS[1], "PS1"), (PS[0], "PS0"), (PS[4], "PS4"))
                for q in range(4):
                    cb, cbk = cbanks[q]
                    P.add("pe", lambda eng, q=q, c=c, cb=cb: eng.matmul(
                        cb[0:1, 0:512], DCOL[:, 0:1],
                        XIT[:, c, q * 512:(q + 1) * 512], start=True, stop=False),
                        reads=(f"xit{c}", "DCOL"), writes=(cbk,))
                    P.add("pe", lambda eng, q=q, c=c, cb=cb,
                          last=(cg == 0): eng.matmul(
                        cb[0:1, 0:512], DCOL[:, 1:2],
                        XIT[:, c, q * 512:(q + 1) * 512], start=False, stop=last),
                        reads=(f"xit{c}", "DCOLb"), writes=(cbk,))
                    if cg > 0:
                        srow = (lambda q=q, cur=cur: WR[0:1, cur + q * 512:cur + (q + 1) * 512])
                        P.add("pe", lambda eng, srow=srow, cb=cb: eng.matmul(
                            cb[0:1, 0:512], P128[0:1, 0:1], srow(),
                            start=False, stop=False),
                            reads=("car", "P128"), writes=(cbk,))
                        P.add("pe", lambda eng, srow=srow, cb=cb: eng.matmul(
                            cb[0:1, 0:512], P128[0:1, 1:2], srow(),
                            start=False, stop=True),
                            reads=("car", "P128b"), writes=(cbk,))
                for q in range(4):
                    cb, cbk = cbanks[q]
                    dstrow = (lambda q=q, nxt=nxt: WR[0:1, nxt + q * 512:nxt + (q + 1) * 512])
                    P.add("dve", lambda eng, dstrow=dstrow, cb=cb: eng.tensor_copy(
                        dstrow(), cb[0:1, 0:512]),
                        reads=(cbk,), writes=("car",))
            for m in range(NF):
                pt = PS[4] if m % 2 == 0 else PS[0]
                bank = "PS4" if m % 2 == 0 else "PS0"
                for g in range(NDI):
                    w = WOUTA[:, g, :] if g < 8 else WOUTB[:, g - 8, :]
                    P.add("pe", lambda eng, m=m, g=g, w=w, pt=pt: eng.matmul(
                        pt[:, 0:SSS], w[:, m * 128:(m + 1) * 128], YF[:, g, :],
                        start=(g == 0), stop=(g == NDI - 1)),
                        reads=(f"yf{g}", f"WOUT{g}"), writes=(bank,))
                sl = m % 2
                P.add("dve", lambda eng, m=m, pt=pt, sl=sl: eng.tensor_scalar(
                    STATS[:, sl * 256:(sl + 1) * 256], pt[:, 0:SSS],
                    BCOL[:, 32 + m:33 + m], None, ALU.add),
                    reads=(bank, "col32", f"ssmst{sl}"), writes=(f"sso{sl}",))
                P.add("sp", lambda eng, m=m, s=s, sl=sl: eng.dma_start(
                    out=ssm_scr[m, :, s * SSS:(s + 1) * SSS],
                    in_=STATS[:, sl * 256:(sl + 1) * 256]),
                    reads=(f"sso{sl}",), writes=(f"ssmst{sl}",), dma=True)
        P.barrier()

    def emit_attn(l):
        for j in range(NF):
            dmaop("pool", WQ[:, j, :], D["Wqkv"][l, j * 128:(j + 1) * 128, :], f"WQ{j}")
        dmaop("sp", QKCOL[0:64, 0:1], D["qsc"][l].rearrange("(a b) -> a b", a=64), "qs1")
        dmaop("sp", QKCOL[64:128, 0:1], D["qsc"][l].rearrange("(a b) -> a b", a=64), "qs2")
        dmaop("sp", QKCOL[0:64, 1:2], D["ksc"][l].rearrange("(a b) -> a b", a=64), "ks1")
        dmaop("sp", QKCOL[64:128, 1:2], D["ksc"][l].rearrange("(a b) -> a b", a=64), "ks2")
        qsk = ("qs1", "qs2")
        ksk = ("ks1", "ks2")
        P.add("dve", lambda eng: eng.tensor_scalar(
            R2QS, R2B[:], QKCOL[:, 0:1], None, ALU.mult),
            reads=("R2B", "qs1", "qs2"), writes=("R2QS",))
        P.add("dve", lambda eng: eng.tensor_scalar(
            R2KS, R2B[:], QKCOL[:, 1:2], None, ALU.mult),
            reads=("R2B", "ks1", "ks2"), writes=("R2KS",))

        for s in range(NS):
            for ti in range(4):
                dmaop("sp", TABR[:, ti, :], tab_scr[ti, :, s * SS:(s + 1) * SS],
                      f"tab{ti}", rkeys=("tabdone",))
            castk = tuple(f"acst{f}" for f in range(NF))
            for f in range(NF):
                P.add("act", lambda eng, f=f, s=s: copy_to(eng,
                    BIGA[:, f, :], Rb[:, f, shifted_cols(f, s)]),
                    reads=rk_shift(f, s), writes=(f"acst{f}",))

            for which in ("q", "k"):
                base = 0 if which == "q" else 1024
                ntgt = PS[4] if which == "q" else PS[5]
                nbank = "PS4" if which == "q" else "PS5"
                irow = 0 if which == "q" else 32
                ikey = "invq" if which == "q" else "invk"
                scol = QKCOL[:, 0:1] if which == "q" else QKCOL[:, 1:2]
                sck = qsk if which == "q" else ksk
                evr = QSR if which == "q" else KSR
                evk = "qsr" if which == "q" else "ksr"
                rot_l = R2QS if which == "q" else R2KS
                rot_k = "R2QS" if which == "q" else "R2KS"
                def proj_stage(g, base=base, evr=evr, evk=evk):
                    bank = f"PS{g % 2}"
                    for j in range(NF):
                        P.add("pe", lambda eng, g=g, j=j, base=base: eng.matmul(
                            PS[g % 2][:, 0:512], WQ[:, j, base + g * 128:base + (g + 1) * 128],
                            BIGA[:, j, :], start=(j == 0), stop=(j == NF - 1)),
                            reads=(f"WQ{j}", castk[j]), writes=(bank,))
                    P.add("dve", lambda eng, g=g, evr=evr: eng.tensor_copy(
                        evr[:, g % 2, :], PS[g % 2][:, 0:512]),
                        reads=(bank,), writes=(f"{evk}{g % 2}",))

                def post_stage(g, which=which, evr=evr, evk=evk, ntgt=ntgt,
                               nbank=nbank, irow=irow, rot_l=rot_l, rot_k=rot_k,
                               scol=scol, sck=sck, s=s):
                    P.add("act", lambda eng, g=g, evr=evr: eng.activation(
                        SCR[:, g % 2, :], evr[:, g % 2, :], AF.Square),
                        reads=(f"{evk}{g % 2}",), writes=(f"SCR{g % 2}",))
                    P.add("pe", lambda eng, g=g, ntgt=ntgt, irow=irow: eng.matmul(
                        ntgt[irow:irow + 16, 0:512], BDB[:, g, :], SCR[:, g % 2, :],
                        start=(g == 0), stop=(g == NF - 1)),
                        reads=(f"SCR{g % 2}", "BDB"), writes=(nbank,))
                    P.add("pe", lambda eng, g=g, rot_l=rot_l, evr=evr: eng.matmul(
                        PS[2][:, 0:512], rot_l, evr[:, g % 2, :], start=True, stop=True),
                        reads=(rot_k, f"{evk}{g % 2}"), writes=("PS2",))
                    tsin = 1 if which == "q" else 3
                    tcos = 0 if which == "q" else 2
                    P.add("dve", lambda eng, g=g, tsin=tsin: eng.tensor_tensor(
                        T2R[:, g % 2, :], PS[2][:, 0:512], TABR[:, tsin, :], ALU.mult),
                        reads=("PS2", f"tab{tsin}"), writes=(f"t2{g % 2}",))
                    if which == "q":
                        P.add("dve", lambda eng, g=g, tcos=tcos, scol=scol, evr=evr:
                              eng.scalar_tensor_tensor(
                                  QPR[:, g, :], evr[:, g % 2, :], scol, TABR[:, tcos, :],
                                  ALU.mult, ALU.mult),
                              reads=(f"{evk}{g % 2}", f"tab{tcos}") + sck, writes=(f"qp{g}",))
                        P.add("dve", lambda eng, g=g: eng.tensor_tensor(
                            QPR[:, g, :], QPR[:, g, :], T2R[:, g % 2, :], ALU.add),
                            reads=(f"t2{g % 2}", f"qp{g}"), writes=(f"qp{g}",))
                    else:
                        kp = f"kp{g}_{s % 2}"
                        kcols = slice((s % 2) * 512, (s % 2) * 512 + 512)
                        P.add("dve", lambda eng, g=g, kcols=kcols, tcos=tcos, scol=scol, evr=evr:
                              eng.scalar_tensor_tensor(
                                  KR[:, g, kcols], evr[:, g % 2, :], scol, TABR[:, tcos, :],
                                  ALU.mult, ALU.mult),
                              reads=(f"{evk}{g % 2}", f"tab{tcos}") + sck, writes=(kp,))
                        P.add("dve", lambda eng, g=g, kcols=kcols: eng.tensor_tensor(
                            KR[:, g, kcols], KR[:, g, kcols], T2R[:, g % 2, :], ALU.add),
                            reads=(f"t2{g % 2}", kp), writes=(kp,))

                proj_stage(0)
                for g in range(1, NF):
                    proj_stage(g)
                    post_stage(g - 1)
                post_stage(NF - 1)
                P.add("dve", lambda eng, ntgt=ntgt, irow=irow: eng.tensor_scalar(
                    INV[irow:irow + 16, :], ntgt[irow:irow + 16, 0:512], 1e-30, None, ALU.add),
                    reads=(nbank,), writes=(ikey,))
                P.add("act", lambda eng, irow=irow: eng.activation(
                    INV[irow:irow + 16, :], INV[irow:irow + 16, :], AF.Ln),
                    reads=(ikey,), writes=(ikey,))
                P.add("act", lambda eng, irow=irow: eng.activation(
                    INV[irow:irow + 16, :], INV[irow:irow + 16, :], AF.Exp, scale=np.float32(0.5)),
                    reads=(ikey,), writes=(ikey,))
                P.add("dve", lambda eng, irow=irow: eng.tensor_scalar(
                    INV[irow:irow + 16, :], INV[irow:irow + 16, :], 1e-12, None, ALU.max),
                    reads=(ikey,), writes=(ikey,))
                P.add("dve", lambda eng, irow=irow: eng.reciprocal(
                    INV[irow:irow + 16, :], INV[irow:irow + 16, :]),
                    reads=(ikey,), writes=(ikey,))
            P.add("dve", lambda eng: eng.tensor_copy(
                SCR[32:48, 0, :], INV[32:48, :]),
                reads=("invk",), writes=("SCR0",))
            for g in range(NF):
                pt = PS[4] if g % 2 == 0 else PS[5]
                bank = "PS4" if g % 2 == 0 else "PS5"
                kp = f"kp{g}_{s % 2}"
                kcols = slice((s % 2) * 512, (s % 2) * 512 + 512)
                P.add("pe", lambda eng, g=g, pt=pt: eng.matmul(
                    pt[:, 0:512], SELB[32:48, g, :], SCR[32:48, 0, :], start=True, stop=True),
                    reads=("SCR0", "SELB"), writes=(bank,))
                P.add("dve", lambda eng, g=g, pt=pt: eng.tensor_copy(
                    IVB[:, g % 2, :], pt[:, 0:512]),
                    reads=(bank,), writes=(f"ivb{g % 2}",))
                P.add("dve", lambda eng, g=g, kcols=kcols: eng.tensor_tensor(
                    KR[:, g, kcols], KR[:, g, kcols], IVB[:, g % 2, :], ALU.mult),
                    reads=(f"ivb{g % 2}", kp), writes=(kp,))
            for c in range(4):
                P.add("pe", lambda eng, c=c: eng.transpose(
                    PS[4][:, 0:16], INV[0:16, c * 128:(c + 1) * 128], IDFB[0:16, 0:16]),
                    reads=("invq", "IDF"), writes=("PS4",))
                P.add("dve", lambda eng, c=c: eng.tensor_scalar(
                    IVQC[:, c, :], PS[4][:, 0:16], 8.0, None, ALU.mult),
                    reads=("PS4",), writes=(f"ivqc{c}",))
            P.add("dve", lambda eng: eng.engine_nop(),
                  reads=("tab0", "tab1", "tab2", "tab3"), writes=("tabdone",))

            for c in range(4):
                tg = s * 4 + c
                for vc in range(2):
                    bank = f"PS{vc}"
                    for j in range(NF):
                        P.add("pe", lambda eng, c=c, j=j, vc=vc: eng.matmul(
                            PS[vc][:, 0:512], BIGA[:, j, c * 128:(c + 1) * 128],
                            WQ[:, j, 2048 + vc * 512:2048 + (vc + 1) * 512],
                            start=(j == 0), stop=(j == NF - 1)),
                            reads=(f"WQ{j}", castk[j]), writes=(bank,))
                    P.add("dve",
                          lambda eng, vc=vc, tg=tg: copy_to(eng,
                              VR[:, tg % 6, vc * 512:(vc + 1) * 512], PS[vc][:, 0:512]),
                          reads=(bank,), writes=(f"vr{tg % 6}",))

            for c in range(4):
                tg = s * 4 + c
                kind = min(tg, 2)
                blocks = [(bi, tg - 2 + bi) for bi in range(3) if tg - 2 + bi >= 0]
                runs = []
                for bi, kb in blocks:
                    rc = kb % 8
                    if runs and runs[-1][1] + runs[-1][2] == rc:
                        runs[-1][2] += 1
                    else:
                        runs.append([bi, rc, 1])
                nvalid = sum(r[2] for r in runs)
                lo = runs[0][0] * 128
                hi = lo + nvalid * 128

                def sbank(w, hh):
                    i = (2 + hh) if w % 2 == 0 else hh
                    return PS[i], f"PS{i}"

                def smslot(w, hh):
                    buf = STATS if w % 2 == 0 else OUTB
                    return buf[:, hh * 384:(hh + 1) * 384], f"SMS{w % 2}{hh}"

                def emit_S(w):
                    for hh in range(2):
                        h = w * 2 + hh
                        g, pbs = h // 2, 64 * (h % 2)
                        Sb, bank = sbank(w, hh)
                        for ri, (bi, rc, ln) in enumerate(runs):
                            P.add("pe", lambda eng, Sb=Sb, g=g, pbs=pbs, c=c, bi=bi,
                                  rc=rc, ln=ln, first=(ri == 0),
                                  last=(ri == len(runs) - 1): eng.matmul(
                                Sb[:, bi * 128:(bi + ln) * 128],
                                QPR[pbs:pbs + 64, g, c * 128:(c + 1) * 128],
                                KR[pbs:pbs + 64, g, rc * 128:(rc + ln) * 128],
                                start=first, stop=last),
                                reads=(f"qp{g}",) + tuple(
                                    f"kp{g}_{((rc + x) // 4) % 2}" for x in range(ln)),
                                writes=(bank,))

                def emit_softmax_head(w, hh):
                    rr = w % 2
                    h = w * 2 + hh
                    Sb, bank = sbank(w, hh)
                    SMS, smk = smslot(w, hh)
                    P.add("dve", lambda eng, Sb=Sb, SMS=SMS, kind=kind, lo=lo, hi=hi:
                          eng.tensor_tensor(SMS[:, lo:hi], Sb[:, lo:hi],
                                            MSKF[:, kind, lo:hi], ALU.add),
                          reads=(bank, "MSKF"), writes=(smk,))
                    if lo > 0:
                        P.add("dve", lambda eng, SMS=SMS, kind=kind, lo=lo:
                              eng.tensor_copy(SMS[:, 0:lo], MSKF[:, kind, 0:lo]),
                              reads=("MSKF",), writes=(smk,))
                    P.add("act", lambda eng, SMS=SMS, rr=rr, hh=hh, c=c, h=h:
                          eng.activation(
                              PSBr[:, rr, hh, :], SMS[:, 0:384], AF.Exp,
                              scale=IVQC[:, c, h:h + 1],
                              accum_out=SUMS[:, h:h + 1]),
                          reads=(smk, f"ivqc{c}"),
                          writes=(f"psb{rr}{hh}", f"sums{h}"))

                def emit_pdiv(w, hh):
                    rr = w % 2
                    h = w * 2 + hh
                    P.add("dve", lambda eng, h=h: eng.reciprocal(
                        OSC[:, h:h + 1], SUMS[:, h:h + 1]),
                        reads=(f"sums{h}",), writes=(f"osc{h}",))
                    if hh == 0:
                        P.add("dve", lambda eng, rr=rr, hh=hh, h=h: eng.tensor_scalar(
                            PSBr[:, rr, hh, :], PSBr[:, rr, hh, :], OSC[:, h:h + 1],
                            None, ALU.mult),
                            reads=(f"osc{h}", f"psb{rr}{hh}"), writes=(f"psb{rr}{hh}",))
                    else:
                        P.add("act", lambda eng, rr=rr, hh=hh, h=h: eng.activation(
                            PSBr[:, rr, hh, :], PSBr[:, rr, hh, :], AF.Identity,
                            scale=OSC[:, h:h + 1]),
                            reads=(f"osc{h}", f"psb{rr}{hh}"), writes=(f"psb{rr}{hh}",))

                def emit_tail(w):
                    rr = w % 2
                    pbk = f"PB{w % 2}"
                    for hh in range(2):
                        for cc in range(3):
                            P.add("pe", lambda eng, w=w, rr=rr, hh=hh, cc=cc: eng.transpose(
                                PB[w % 2][:, hh * 384 + cc * 128:hh * 384 + (cc + 1) * 128],
                                PSBr[:, rr, hh, cc * 128:(cc + 1) * 128], IDB[:]),
                                reads=(f"psb{rr}{hh}", "IDB"), writes=(pbk,))
                    P.add("dve",
                          lambda eng, w=w, rr=rr: copy_to(eng,
                              PTBr[:, rr, :, :].rearrange("p a b -> p (a b)"),
                              PB[w % 2][:, 0:768]),
                          reads=(pbk,), writes=(f"ptb{rr}0", f"ptb{rr}1"))
                    for hh in range(2):
                        h = w * 2 + hh
                        pr = h // 2
                        Ob = PS[4] if pr < 4 else PS[5]
                        obank = f"PS{4 if pr < 4 else 5}"
                        for ii, (bi, kb) in enumerate(blocks):
                            P.add("pe", lambda eng, Ob=Ob, h=h, pr=pr, bi=bi, kb=kb, hh=hh,
                                  rr=rr,
                                  first=(ii == 0), last=(ii == len(blocks) - 1): eng.matmul(
                                Ob[64 * (h % 2):64 * (h % 2) + 64,
                                   (pr % 4) * 128:(pr % 4) * 128 + 128],
                                VR[:, kb % 6, h * 64:(h + 1) * 64],
                                PTBr[:, rr, hh, bi * 128:(bi + 1) * 128],
                                start=first, stop=last),
                                reads=(f"vr{kb % 6}", f"ptb{rr}{hh}"), writes=(obank,))

                emit_S(0)
                for w in range(8):
                    emit_softmax_head(w, 0)
                    emit_softmax_head(w, 1)
                    emit_pdiv(w, 0)
                    emit_pdiv(w, 1)
                    if w < 7:
                        emit_S(w + 1)
                    emit_tail(w)
                for pr in range(8):
                    Ob = PS[4] if pr < 4 else PS[5]
                    obank = f"PS{4 if pr < 4 else 5}"
                    P.add("dve",
                          lambda eng, pr=pr, c=c, Ob=Ob: copy_to(eng,
                              OFMv[:, pr, (c % 2) * 128:(c % 2) * 128 + 128],
                              Ob[:, (pr % 4) * 128:(pr % 4) * 128 + 128]),
                          reads=(obank,), writes=(f"ofm{pr}",))
                if c % 2 == 1:
                    for m in range(NF):
                        wr = (s * 2 + c // 2) % 2 if False else ((s * 8 + (c // 2) * 4 + m) % 2)
                        dmaop("pool", WOB[:, wr, :].rearrange("p (j c) -> p j c", j=8),
                              D["Wo"][l, :, m * 128:(m + 1) * 128]
                              .rearrange("(j p) c -> p j c", p=128),
                              f"wob{wr}", rkeys=(f"wodone{wr}",))
                        bank = f"PS{m % 2}"
                        for j in range(NF):
                            P.add("pe", lambda eng, m=m, j=j, wr=wr: eng.matmul(
                                PS[m % 2][:, 0:256], WOB[:, wr, j * 128:(j + 1) * 128],
                                OFMv[:, j, :], start=(j == 0), stop=(j == NF - 1)),
                                reads=(f"wob{wr}",) + tuple(f"ofm{p2}" for p2 in range(8)),
                                writes=(bank,))
                        ccols = slice(1 + s * 512 + (c // 2) * 256, 1 + s * 512 + (c // 2) * 256 + 256)
                        P.add("dve", lambda eng, m=m, s=s, ccols=ccols, wr=wr: eng.tensor_tensor(
                            Rb[:, m, ccols], Rb[:, m, ccols], PS[m % 2][:, 0:256], ALU.add),
                            reads=(bank, f"R{m}.{s}"), writes=(f"R{m}.{s}", f"wodone{wr}"))
        P.barrier()

    def emit_ffn(l):
        for kc in range(NKC):
            kn = 128 if kc < NKC - 1 else 42
            dmaop("pool", W2R[0:kn, kc, :], D["W2"][l, kc * 128:kc * 128 + kn, :], f"w2{kc}")
        dmaop("pool", W2R[42:43, NKC - 1, :], D["b2"][l].rearrange("(a c) -> a c", a=1), "b2row")
        dmaop("pool", MH[42:43, NKC - 1, :], D["ones_512"], "onesm")
        dmaop("sp", OUTB[0:21, 0:128], D["b1"][l, 0:2688].rearrange("(a b) -> a b", a=21),
              ("b1ar", "OUTB"), rkeys=("PS0",))
        dmaop("sp", OUTB[21:22, 0:42], D["b1"][l, 2688:2730].rearrange("(a c) -> a c", a=1),
              ("b1ar2", "OUTB"), rkeys=("PS0",))
        P.add("pe", lambda eng: eng.transpose(PS[0][:, 0:22], OUTB[0:22, 0:128], IDFB[0:22, 0:22]),
              reads=("b1ar", "b1ar2", "IDF", "OUTB"), writes=("PS0",))
        P.add("dve", lambda eng: eng.tensor_copy(B1A[:, :], PS[0][:, 0:22]),
              reads=("PS0",), writes=("b1a",))
        dmaop("sp", OUTB[0:21, 0:128], D["b1"][l, 2730:5418].rearrange("(a b) -> a b", a=21),
              ("b1gr", "OUTB"), rkeys=("PS0",))
        dmaop("sp", OUTB[21:22, 0:42], D["b1"][l, 5418:5460].rearrange("(a c) -> a c", a=1),
              ("b1gr2", "OUTB"), rkeys=("PS0",))
        P.add("pe", lambda eng: eng.transpose(PS[0][:, 0:22], OUTB[0:22, 0:128], IDFB[0:22, 0:22]),
              reads=("b1gr", "b1gr2", "IDF", "OUTB"), writes=("PS0",))
        P.add("dve", lambda eng: eng.tensor_copy(B1G[:, :], PS[0][:, 0:22]),
              reads=("PS0",), writes=("b1g",))
        for nm, coff, key in (("lng", 0, "lngc"), ("lnb", 8, "lnbc")):
            dmaop("sp", OUTB[0:8, 0:128], D[nm][l].rearrange("(a b) -> a b", a=8),
                  (f"{nm}r", "OUTB"), rkeys=("PS0",))
            P.add("pe", lambda eng, nm=nm: eng.transpose(
                PS[0][:, 0:8], OUTB[0:8, 0:128], IDFB[0:8, 0:8]),
                reads=(f"{nm}r", "IDF", "OUTB"), writes=("PS0",))
            P.add("dve", lambda eng, coff=coff: eng.tensor_copy(
                LNGB[:, coff:coff + 8], PS[0][:, 0:8]),
                reads=("PS0",), writes=(key,))

        for hf in (1, 0):
            for si in range(2):
                s = hf * 2 + si
                emit_ln_stats_rows(s, shifted=True, rk=f"fl{s}")
                for f in range(NF):
                    P.add("dve", lambda eng, f=f, s=s, si=si: eng.tensor_tensor(
                        SCR[:, si, :], Rb[:, f, shifted_cols(f, s)], PS[4][:, 0:512],
                        ALU.subtract),
                        reads=("PS4",) + rk_shift(f, s), writes=(f"SCR{si}",))
                    P.add("dve", lambda eng, f=f, si=si: eng.tensor_tensor(
                        SCR[:, si, :], SCR[:, si, :], PS[5][:, 0:512], ALU.mult),
                        reads=("PS5",), writes=(f"SCR{si}",))
                    P.add("act", lambda eng, f=f, si=si: eng.activation(
                        XNH[:, f, si * 512:(si + 1) * 512], SCR[:, si, :],
                        AF.Identity, scale=LNGB[:, f:f + 1], bias=LNGB[:, 8 + f:9 + f]),
                        reads=(f"SCR{si}", "lngc", "lnbc"), writes=(f"xn{f}",))
            for kc in range(NKC):
                kn = 128 if kc < NKC - 1 else 42
                wr = kc % 2
                for ab in range(2):
                    cbase = ab * DF + kc * 128
                    dmaop("pool", W1R[:, wr, ab, :, 0:kn],
                          D["W1"][l, :, cbase:cbase + kn].rearrange("(j p) c -> p j c", p=128),
                          f"w1{wr}{ab}", rkeys=(f"w1d{wr}{ab}",))
                for si in range(2):
                    bb = ((kc * 2 + si) % 2) * 2
                    for ab in range(2):
                        bank = f"PS{bb + ab}"
                        for j in range(NF):
                            P.add("pe", lambda eng, wr=wr, ab=ab, j=j, si=si, kn=kn,
                                  bb=bb: eng.matmul(
                                PS[bb + ab][0:kn, 0:512], W1R[:, wr, ab, j, 0:kn],
                                XNH[:, j, si * 512:(si + 1) * 512],
                                start=(j == 0), stop=(j == NF - 1)),
                                reads=(f"w1{wr}{ab}", f"xn{j}"), writes=(bank,))
                    P.add("dve", lambda eng, kc=kc, si=si, kn=kn, bb=bb: eng.tensor_scalar(
                        SCR[0:kn, si, :], PS[bb + 1][0:kn, 0:512],
                        B1G[0:kn, kc:kc + 1], None, ALU.add),
                        reads=(f"PS{bb + 1}", "b1g"), writes=(f"SCR{si}",))
                    P.add("act", lambda eng, kc=kc, si=si, kn=kn: eng.activation(
                        BIGA[0:kn, si, 0:512], SCR[0:kn, si, :], AF.Gelu),
                        reads=(f"SCR{si}",), writes=(f"gel{si}",))
                    wx = (f"w1d{wr}0", f"w1d{wr}1") if si == 1 else ()
                    P.add("dve", lambda eng, kc=kc, si=si, hf=hf, kn=kn, bb=bb:
                          eng.scalar_tensor_tensor(
                              MH[0:kn, kc, si * 512:(si + 1) * 512], PS[bb][0:kn, 0:512],
                              B1A[0:kn, kc:kc + 1], BIGA[0:kn, si, 0:512],
                              ALU.add, ALU.mult),
                          reads=(f"PS{bb}", "b1a", f"gel{si}"),
                          writes=(f"m{kc}",) + wx)
            for si in range(2):
                s = hf * 2 + si
                for m in range(NF):
                    bank = f"PS{m % 2}"
                    for kc in range(NKC):
                        kn = 128 if kc < NKC - 1 else 43
                        P.add("pe", lambda eng, m=m, kc=kc, si=si, kn=kn: eng.matmul(
                            PS[m % 2][:, 0:512], W2R[0:kn, kc, m * 128:(m + 1) * 128],
                            MH[0:kn, kc, si * 512:(si + 1) * 512],
                            start=(kc == 0), stop=(kc == NKC - 1)),
                            reads=(f"w2{kc}", f"m{kc}") +
                                  (("b2row", "onesm") if kc == NKC - 1 else ()),
                            writes=(bank,))
                    P.add("dve", lambda eng, m=m, s=s: eng.tensor_tensor(
                        Rb[:, m, SL(s)], Rb[:, m, SL(s)], PS[m % 2][:, 0:512], ALU.add),
                        reads=(bank, f"R{m}.{s}"), writes=(f"R{m}.{s}",))
        P.barrier()

    for l in range(layers):
        emit_attn(l)
        if not skip_ffn:
            emit_ffn(l)

    if add_ssm:
        for j in range(NT):
            s4 = j // 4
            P.add("sp", lambda eng, j=j: eng.dma_start(
                out=OUTB[:].rearrange("p (f c) -> p f c", f=8),
                in_=ssm_scr[:, :, j * 128:(j + 1) * 128].rearrange("f p c -> p f c")),
                reads=("fssc",), writes=("fss", "OUTB"), dma=True)
            for f in range(NF):
                P.add("dve", lambda eng, f=f, j=j: eng.tensor_tensor(
                    Rb[:, f, 1 + j * 128:1 + (j + 1) * 128],
                    Rb[:, f, 1 + j * 128:1 + (j + 1) * 128],
                    OUTB[:].rearrange("p (f c) -> p f c", f=8)[:, f, :], ALU.add),
                    reads=("fss", f"R{f}.{s4}", "OUTB"),
                    writes=(f"R{f}.{s4}",) + (("fssc",) if f == NF - 1 else ()))
    P.barrier()
    for j in range(0 if "final" in _SKIP else NT):
        s4 = j // 4
        if add_x:
            dmaop("sp", OUTB[:], D["x"][j * 128:(j + 1) * 128, :], ("fx", "OUTB"), rkeys=("fstored",))
        for fg in range(2):
            bank = f"PS{fg}"
            for ff in range(4):
                f = fg * 4 + ff
                P.add("pe", lambda eng, f=f, j=j, fg=fg, ff=ff, s4=s4: eng.transpose(
                    PS[fg][:, ff * 128:(ff + 1) * 128],
                    Rb[:, f, 1 + j * 128:1 + (j + 1) * 128], IDFB[:]),
                    reads=(f"R{f}.{s4}", "IDF"), writes=(bank,))
            if add_x:
                P.add("dve", lambda eng, fg=fg: eng.tensor_tensor(
                    OUTB[:, fg * 512:(fg + 1) * 512], PS[fg][:, 0:512],
                    OUTB[:, fg * 512:(fg + 1) * 512], ALU.add),
                    reads=(bank, "fx"), writes=(f"fo{fg}", "OUTB"))
            else:
                P.add("dve", lambda eng, fg=fg: eng.tensor_copy(
                    OUTB[:, fg * 512:(fg + 1) * 512], PS[fg][:, 0:512]),
                    reads=(bank,), writes=(f"fo{fg}", "OUTB"))
        P.add("sp", lambda eng, j=j: eng.dma_start(
            out=OUT[j * 128:(j + 1) * 128, :], in_=OUTB[:]),
            reads=("fo0", "fo1", "OUTB"), writes=("fstored",), dma=True, force_inc=True)

    P.emit(block, sems)
    nc._keepalive_ctx = ctx
    return nc


def kernel(**inputs):
    cfg = (_CFG["layers"], _CFG["add_ssm"], _CFG["add_x"], _CFG.get("skip_ffn", False))
    if cfg not in _NC_CACHE:
        _NC_CACHE[cfg] = build(cfg)
    nc = _NC_CACHE[cfg]
    C = _host_consts()

    shared = {}
    shared["Win"] = inputs["L0_attn_Win"]
    shared["bin"] = inputs["L0_attn_bin_"]
    shared["Bp"] = inputs["L0_attn_Bp"]
    shared["Cp"] = inputs["L0_attn_Cp"]
    shared["Dp"] = inputs["L0_attn_Dp"]
    shared["Wout"] = inputs["L0_attn_Wout"]
    shared["bout"] = inputs["L0_attn_bout"]
    shared["Wqkv"] = inputs["L0_ffn_Wqkv"]
    shared["qsc"] = inputs["L0_ffn_q_scale"]
    shared["ksc"] = inputs["L0_ffn_k_scale"]
    shared["Wo"] = inputs["L0_ffn_Wo"]
    shared["lng"] = inputs["L0_ffn_ln_g"]
    shared["lnb"] = inputs["L0_ffn_ln_b"]
    shared["W1"] = inputs["L0_ffn_W1"]
    shared["b1"] = inputs["L0_ffn_b1"]
    shared["W2"] = inputs["L0_ffn_W2"]
    shared["b2"] = inputs["L0_ffn_b2"]
    for k in ("tab_cos", "tab_sin", "tab_scl", "tab_isc", "R2", "IDF", "MSK", "BD",
              "SEL", "T_hi", "T_lo", "dinj_hi", "dinj_lo", "dcol_hi", "dcol_lo",
              "p128_hi", "p128_lo", "ones_row128", "ones_col", "ones_512"):
        shared[k] = C[k]
    shared = {k: np.ascontiguousarray(np.asarray(v, dtype=np.float32))
              for k, v in shared.items()}

    x = np.asarray(inputs["L0_attn_x"], dtype=np.float32)
    in_maps = []
    for c in range(B):
        m = dict(shared)
        m["x"] = np.ascontiguousarray(x[c])
        in_maps.append(m)

    res = run_bass_kernel_spmd(nc, in_maps, core_ids=list(range(B)))
    return np.stack([np.asarray(r["out"]) for r in res.results], axis=0)
